# version 15
# baseline (speedup 1.0000x reference)
import numpy as np
import concourse.bass as bass
import concourse.mybir as mybir
from concourse.bass_utils import run_bass_kernel_spmd

F32 = mybir.dt.float32
BF16 = mybir.dt.bfloat16
I32 = mybir.dt.int32
AF = mybir.ActivationFunctionType
ALU = mybir.AluOpType
AX = mybir.AxisListType.X

D = 1024
T = 2048
NT = 4
TT = 512
NG = 4
NC = 8
NP = 8
MAGIC = 12582912.0
PROBE = 0.3


class _MockIns:
    def then_inc(self, *a, **kw):
        return self


class _CountEng:

    def __init__(self):
        self.n = 0

    def wait_ge(self, *a, **kw):
        pass

    def sem_inc(self, *a, **kw):
        pass

    def __getattr__(self, name):
        def f(*a, **kw):
            self.n += 1
            return _MockIns()
        return f


class _IncEng:

    def __init__(self, eng, sem, inc_n):
        self._eng = eng
        self._sem = sem
        self._inc_n = inc_n
        self.n = 0

    def wait_ge(self, *a, **kw):
        return self._eng.wait_ge(*a, **kw)

    def sem_inc(self, *a, **kw):
        return self._eng.sem_inc(*a, **kw)

    def __getattr__(self, name):
        f = getattr(self._eng, name)
        if not callable(f):
            return f

        def wrap(*a, **kw):
            ins = f(*a, **kw)
            if ins is not None and hasattr(ins, "then_inc"):
                ins.then_inc(self._sem, self._inc_n)
                self.n += 1
            return ins
        return wrap


class Sched:

    def __init__(self, nc):
        self.nc = nc
        self.ops = []
        self.evset = set()
        self.alias = {}
        self.sorted_ops = None

    def sem(self, name):
        return self.nc.alloc_semaphore(name)

    def op(self, engine, emit, waits=(), inc=None, event=None, inc_n=1,
           key=None):
        for w in waits:
            assert w in self.evset, f"unknown event {w}"
        mock = _CountEng()
        emit(mock)
        n_ins = mock.n
        if event is not None:
            assert inc is not None
            assert event not in self.evset, f"dup event {event}"
            self.evset.add(event)
        if key is None:
            key = float(len(self.ops))
        self.ops.append([key, engine, emit, tuple(waits), inc, inc_n,
                         n_ins, event])

    def alias_event(self, name, target):
        assert target in self.evset and name in self.evset
        self.alias[name] = target

    def finalize(self):
        self.sorted_ops = sorted(self.ops, key=lambda t: t[0])
        counts = {}
        self.evval = {}
        for key, eng, emit, wn, inc, inc_n, n_ins, event in self.sorted_ops:
            if inc is not None:
                c = counts.get(inc.name, 0) + inc_n * max(n_ins, 1)
                counts[inc.name] = c
                if event is not None:
                    self.evval[event] = (inc, c)
        for name, target in self.alias.items():
            self.evval[name] = self.evval[target]

    def emit_engine(self, engine, eng):
        if self.sorted_ops is None:
            self.finalize()
        for key, e, emit, wn, inc, inc_n, n_ins, event in self.sorted_ops:
            if e != engine:
                continue
            best = {}
            for w in wn:
                sem, v = self.evval[w]
                if sem.name not in best or best[sem.name][1] < v:
                    best[sem.name] = (sem, v)
            for sem, v in best.values():
                eng.wait_ge(sem, v)
            if inc is not None and n_ins > 0:
                prox = _IncEng(eng, inc, inc_n)
                emit(prox)
                assert prox.n == n_ins, f"emit count {prox.n} != {n_ins}"
            else:
                emit(eng)
                if inc is not None:
                    eng.sem_inc(inc, inc_n)


def build_nc():
    nc = bass.Bass("TRN2", target_bir_lowering=False,
                   detect_race_conditions=False)

    dx_fm = nc.dram_tensor("x_fm", [D, T], F32, kind="ExternalInput")
    dx_tm = nc.dram_tensor("x_tm", [T, D], F32, kind="ExternalInput")
    dws = {w: nc.dram_tensor(w, [D, D], F32, kind="ExternalInput")
           for w in ("wq_t", "wk_t", "wv_t", "wu_t", "wo_t", "ffn_w")}
    dfb = nc.dram_tensor("ffn_b", [128, 8], F32, kind="ExternalInput")
    dc_in = nc.dram_tensor("c_vec", [128, 8], F32, kind="ExternalInput")
    dwm = nc.dram_tensor("wmod", [D, 6 * D], F32, kind="ExternalInput")
    did_b = nc.dram_tensor("ident_bf", [128, 128], BF16, kind="ExternalInput")
    did_f = nc.dram_tensor("ident_f", [128, 128], F32, kind="ExternalInput")
    don_f = nc.dram_tensor("ones_f", [1, 128], F32, kind="ExternalInput")
    dzc_b = nc.dram_tensor("zcols_b", [128, 3], BF16, kind="ExternalInput")
    dmss = nc.dram_tensor("msk_ss", [NP, 128, 16], BF16, kind="ExternalInput")
    dmbc = nc.dram_tensor("msk_bc", [NP, 16, 128], BF16, kind="ExternalInput")
    dsel = nc.dram_tensor("sel8", [8, 8 * 128], BF16, kind="ExternalInput")
    dselr = nc.dram_tensor("sel33b", [33, 2], F32, kind="ExternalInput")
    dout = nc.dram_tensor("out_fm", [NC, 128, NT, TT], F32, kind="ExternalOutput")

    def dap(t, offset, dims):
        return bass.AP(t, offset, dims)

    BASE, CAP = 17440, 229368
    cur = [BASE]
    szof = {F32: 4, BF16: 2, I32: 4}

    def sb(name, shape, dtype, at=None):
        nbytes = int(np.prod(shape[1:])) * szof[dtype]
        off = cur[0] if at is None else at
        if at is None:
            cur[0] = off + ((nbytes + 31) // 32) * 32
        assert off + nbytes <= CAP, f"SBUF overflow: {name} @{off}+{nbytes}"
        return nc.alloc_sbuf_tensor_at(name, list(shape), dtype, offset=off)

    q_sb = sb("q_sb", [128, NC * T], BF16)
    u_sb = sb("u_sb", [128, NC * T], BF16)
    W0 = cur[0]
    wqT = sb("wqT", [128, NC * D], BF16)
    wkT = sb("wkT", [128, NC * D], BF16)
    wvT = sb("wvT", [128, NC * D], BF16)
    wuT = sb("wuT", [128, NC * D], BF16)
    woT = sb("woT", [128, NC * D], BF16, at=W0)
    fqT = sb("fqT", [128, NC * D], BF16, at=W0 + 16384)
    fwS = sb("fwS", [128, NC * D], F32, at=W0 + 32768)
    kv_sb = sb("kv_sb", [128, NP * 128], BF16)
    id_b = sb("id_b", [128, 128], BF16)
    id_f = sb("id_f", [128, 128], F32)
    on_f = sb("on_f", [1, 128], F32)
    zc_b = sb("zc_b", [128, 3], BF16)
    mss = sb("mss", [128, NP * 16], BF16)
    mbc = sb("mbc", [16, NP * 128], BF16)
    sel8 = sb("sel8", [8, 8 * 128], BF16)
    selr = sb("sel33b", [33, 2], F32)
    c_col = sb("c_col", [128, 8], F32)
    silc = sb("silc", [128, 8], F32)
    bias_c = sb("bias_c", [128, 8], F32)
    ca1 = sb("ca1", [128, 8], F32)
    sa_c = sb("sa_c", [128, 8], F32)
    ga_c = sb("ga_c", [128, 8], F32)
    cb1 = sb("cb1", [128, 8], F32)
    sb_c = sb("sb_c", [128, 8], F32)
    gb_c = sb("gb_c", [128, 8], F32)
    gsgn = sb("gsgn", [128, 16], F32)
    half_c = sb("half_c", [128, 1], F32)
    probe_s = sb("probe_s", [1, 2], F32)
    ws_c = sb("ws_c", [128, 8], F32)
    wrs_c = sb("wrs_c", [128, 8], F32)
    gws_c = sb("gws_c", [128, 8], F32)
    gbb_c = sb("gbb_c", [128, 8], F32)
    bn6 = sb("bn6", [128, 48], F32)
    bn2 = sb("bn2", [128, 2 * NG], F32)
    spack = sb("spack", [128, 8], F32)
    lrow = sb("lrow", [8, 128], BF16)
    r16 = sb("r16", [16, TT], BF16)
    rowsB = sb("rowsB", [128, TT], F32)
    ss_s = rowsB
    qrow = sb("qrow", [8, 128], BF16)
    qpack = sb("qpack", [128, 8], F32)
    amx = sb("amx", [128, NG], F32)
    negq = sb("negq", [128, 1], F32)
    ws2 = sb("ws2", [128, 8], F32)
    amx2 = sb("amx2", [128, NG], F32)
    fence = sb("fence", [128, 8], F32)

    U0 = cur[0]
    mrow = sb("mrow", [1, 6 * D], F32, at=U0)
    wmq = sb("wmq", [128, 2 * 4096], F32, at=U0 + 24576)
    x_tm = sb("x_tm", [128, NG * D], BF16, at=U0)
    x_fma = sb("x_fma", [128, NC * TT], BF16, at=U0 + 8192)
    h1r = [sb("h1a", [128, NC * TT], BF16, at=U0 + 16384),
           sb("h1b", [128, NC * TT], BF16, at=U0 + 24576)]
    h1T = sb("h1T", [128, NG * D], BF16, at=U0 + 32768)
    kTM = sb("kTM", [128, D], BF16, at=U0 + 40960)
    vTM = sb("vTM", [128, D], BF16, at=U0 + 43008)
    bcA_mu = sb("bcA_mu", [128, TT], F32, at=U0 + 45056)
    bcA_r = sb("bcA_r", [128, TT], F32, at=U0 + 47104)
    o_bf = sb("o_bf", [128, NP * TT], BF16, at=U0)
    osq = sb("osq", [128, TT], BF16, at=U0 + 8192)
    xq_i = sb("xq_i", [128, TT], I32, at=U0 + 9216)
    sgn2 = sb("sgn2", [128, TT], F32, at=U0 + 11264)
    x2 = sb("x2", [128, NC * TT], F32, at=U0 + 14336)
    h2 = sb("h2", [128, NC * TT], F32, at=U0 + 30720)
    ouxq = sb("ouxq", [128, NC * TT], BF16, at=U0 + 47104)
    xs_sb = sb("xs_sb", [128, TT], F32, at=U0 + 55296)
    x_fmb = sb("x_fmb", [128, NC * 256], F32, at=U0 + 57344)
    M_sb = sb("M_sb", [128, TT], BF16, at=U0 + 13312)
    UEND = U0 + 57344 + 8192
    wsg1 = sb("wsg1", [128, D], BF16, at=U0 + 57344)
    wqi = sb("wqi", [128, D], I32, at=U0 + 59392)
    wqb = sb("wqb", [128, NC * D], BF16, at=U0 + 14336)
    assert UEND <= CAP, f"union overflow {UEND}"

    pb = [nc.alloc_psum_tensor(f"pb{i}", [128, 512], F32) for i in range(8)]

    sch = Sched(nc)
    sD = sch.sem("sD")
    sG = sch.sem("sG")
    sT = sch.sem("sT")
    sV = sch.sem("sV")
    sS = sch.sem("sS")
    sP = sch.sem("sP")
    sO = sch.sem("sO")
    sWq = sch.sem("sWq")
    sWk = sch.sem("sWk")
    sWv = sch.sem("sWv")
    sWu = sch.sem("sWu")
    sWo = sch.sem("sWo")
    sFW = sch.sem("sFW")
    sXT = sch.sem("sXT")
    sXF = sch.sem("sXF")
    sXB = sch.sem("sXB")
    sGb = sch.sem("sGb")

    def ld(dst_ap, src_ap):
        return lambda e: e.dma_start(out=dst_ap, in_=src_ap)

    for ap_, dr, ev in ((id_b[:], did_b, "l_idb"), (id_f[:], did_f, "l_idf"),
                        (on_f[:], don_f, "l_ones"), (zc_b[:], dzc_b, "l_zcb")):
        sch.op("sync", ld(ap_, dr.ap()), inc=sD, event=ev, inc_n=16)
    sch.op("sync", ld(mss[:], dap(dmss, 0, [[16, 128], [128 * 16, NP], [1, 16]])),
           inc=sD, event="l_mss", inc_n=16)
    sch.op("sync", ld(mbc[:], dap(dmbc, 0, [[128, 16], [16 * 128, NP], [1, 128]])),
           inc=sD, event="l_mbc", inc_n=16)
    sch.op("sync", ld(sel8[:], dsel.ap()), inc=sD, event="l_sel", inc_n=16)
    sch.op("sync", ld(selr[:], dselr.ap()), inc=sD, event="l_selr", inc_n=16)
    sch.op("sync", ld(c_col[:], dc_in.ap()), inc=sD, event="l_c", inc_n=16)
    sch.op("sync", ld(bias_c[:], dfb.ap()), inc=sD, event="l_bias", inc_n=16)
    for _ev in ("l_idb", "l_idf", "l_ones", "l_zcb", "l_mss", "l_mbc",
                "l_sel", "l_selr", "l_c"):
        sch.alias_event(_ev, "l_bias")

    NW = 12

    def wm8(i):
        return ld(wmq[:, (i % 2) * 4096:(i % 2 + 1) * 4096],
                  dap(dwm, i * 512, [[6 * D, 128], [128 * 6 * D, NC], [1, 512]]))

    def wfull(dst, name):
        return ld(dst[:], dap(dws[name], 0, [[D, 128], [128 * D, NC], [1, D]]))

    sch.op("gpsimd", wm8(0), inc=sG, event="l_wm0", inc_n=16)
    sch.op("gpsimd", wm8(1), inc=sGb, event="l_wm1", inc_n=16)
    sch.op("gpsimd", wfull(wqT, "wq_t"), inc=sWq, event="l_wq", inc_n=16)

    def probe0(e):
        e.memset(negq[:], -0.25)
        return e.memset(half_c[:], PROBE)
    sch.op("vector", probe0, inc=sV, event="probe_m")
    sch.op("vector", lambda e: e.tensor_scalar(
               out=half_c[:], in0=half_c[:], scalar1=MAGIC, scalar2=MAGIC,
               op0=ALU.add, op1=ALU.subtract),
           waits=["probe_m"], inc=sV, event="probe_r")
    sch.op("vector", lambda e: e.tensor_scalar_mul(
               out=half_c[:], in0=half_c[:], scalar1=0.5 / PROBE),
           waits=["probe_r"], inc=sV, event="half_rdy")

    sch.op("scalar", lambda e: e.activation(out=silc[:], in_=c_col[:], func=AF.Silu),
           waits=["l_c"], inc=sS, event="silc")

    for i in range(NW):
        def mmod(i=i):
            def f(e):
                ins = None
                for dc in range(NC):
                    ins = e.matmul(pb[7][0:1, 0:512], silc[:, dc:dc + 1],
                                   wmq[:, (i % 2) * 4096 + dc * 512:
                                       (i % 2) * 4096 + dc * 512 + 512],
                                   start=(dc == 0), stop=(dc == NC - 1))
                return ins
            return f
        w = ["silc", f"l_wm{i}"]
        if i > 0:
            w.append(f"e_mod_{i - 1}")
        sch.op("tensor", mmod(), waits=w, inc=sT, event=f"m_mod_{i}")

        def emod(i=i):
            def f(e):
                return e.tensor_copy(out=mrow[:, i * 512:(i + 1) * 512],
                                     in_=pb[7][0:1, 0:512])
            return f
        sch.op("vector", emod(), waits=[f"m_mod_{i}"], inc=sV,
               event=f"e_mod_{i}")
        if i + 2 < NW:
            sch.op("gpsimd", wm8(i + 2), waits=[f"e_mod_{i}"],
                   inc=(sG if i % 2 == 0 else sGb),
                   event=f"l_wm{i + 2}", inc_n=16)
        if i == 2:
            sch.op("gpsimd", wfull(wkT, "wk_t"), inc=sWk, event="l_wk", inc_n=16)
        if i == 5:
            sch.op("gpsimd", wfull(wvT, "wv_t"), inc=sWv, event="l_wv", inc_n=16)
        if i == 8:
            sch.op("gpsimd", wfull(wuT, "wu_t"), inc=sWu, event="l_wu", inc_n=16)

    mod_cols = [sa_c, ca1, ga_c, sb_c, cb1, gb_c]
    for vi, col in enumerate(mod_cols):
        for dc in range(NC):
            seg = vi * D + dc * 128
            need = (seg + 127) // 512

            def tmod(seg=seg, dc=dc):
                def f(e):
                    return e.matmul(pb[6][:, (dc % 4) * 128:(dc % 4) * 128 + 128],
                                    mrow[:1, seg:seg + 128], id_f[:1, :],
                                    start=True, stop=True)
                return f
            w = [f"e_mod_{need}", "l_idf"]
            if vi == 0 and dc == 0:
                w.append("half_rdy")
            if dc % 4 == 0 and (vi or dc):
                w.append(f"ec_{vi}_{dc - 4}" if dc else f"ec_{vi - 1}_4")
            sch.op("tensor", tmod(), waits=w, inc=sT, event=f"tc_{vi}_{dc}")
            if dc % 4 == 3:
                def ecol(col=col, dc=dc):
                    def f(e):
                        return e.tensor_copy(
                            out=col[:, dc - 3:dc + 1],
                            in_=bass.AP(pb[6], 0, [[512, 128], [128, 4]]))
                    return f
                sch.op("vector", ecol(), waits=[f"tc_{vi}_{dc}"], inc=sV,
                       event=f"ec_{vi}_{dc - 3}")

    def fin1(e):
        e.tensor_scalar_add(out=ca1[:], in0=ca1[:], scalar1=1.0)
        return e.tensor_scalar_add(out=cb1[:], in0=cb1[:], scalar1=1.0)
    sch.op("vector", fin1, waits=[f"ec_{vi}_4" for vi in range(6)],
           inc=sV, event="fin1")
    def fin2s(e):
        e.activation(out=gsgn[:, 0:8], in_=ga_c[:], func=AF.Sign)
        e.activation(out=gsgn[:, 8:16], in_=gb_c[:], func=AF.Sign)
    sch.op("scalar", fin2s, waits=["fin1", "half_rdy"], inc=sS, event="fin2s")

    def fin2a(e):
        e.activation(out=ga_c[:], in_=ga_c[:], func=AF.Abs)
        return e.activation(out=gb_c[:], in_=gb_c[:], func=AF.Abs)
    sch.op("scalar", fin2a, waits=["fin2s"], inc=sS, event="fin2a")

    def fin2b(e):
        e.activation(out=ga_c[:], in_=ga_c[:], func=AF.Relu, bias=negq[:, 0:1])
        return e.activation(out=gb_c[:], in_=gb_c[:], func=AF.Relu,
                            bias=negq[:, 0:1])
    sch.op("scalar", fin2b, waits=["fin2a"], inc=sS, event="fin2")
    def fin3(e):
        e.scalar_tensor_tensor(out=ga_c[:], in0=ga_c[:], scalar=0.25,
                               in1=gsgn[:, 0:8], op0=ALU.add, op1=ALU.mult)
        return e.scalar_tensor_tensor(out=gb_c[:], in0=gb_c[:], scalar=0.25,
                                      in1=gsgn[:, 8:16], op0=ALU.add,
                                      op1=ALU.mult)
    sch.op("vector", fin3, waits=["fin2"], inc=sV, event="mod_rdy")

    kv_first = [True, True]

    def emit_prepA(i):
        pe = f"A{i}"
        w = ["mod_rdy"] if i == 0 else [f"A{i-1}_bn"]
        sch.op("gpsimd",
               ld(x_tm[:], dap(dx_tm, i * TT * D, [[D, 128], [128 * D, NG], [1, D]])),
               waits=w, inc=sXT, event=f"{pe}_xtm", inc_n=16)
        w = ["mod_rdy"] if i == 0 else [f"A{i-1}_h1"]
        sch.op("gpsimd",
               ld(x_fma[:], dap(dx_fm, i * TT, [[T, 128], [128 * T, NC], [1, TT]])),
               waits=w, inc=sXF, event=f"{pe}_xfm", inc_n=16)

        def bnst(e):
            for g in range(NG):
                e.bn_stats(out=bn6[:, g * 12:g * 12 + 6],
                           in_=x_tm[:, g * D:g * D + 512])
                e.bn_stats(out=bn6[:, g * 12 + 6:g * 12 + 12],
                           in_=x_tm[:, g * D + 512:(g + 1) * D])
        sch.op("vector", bnst, waits=[f"{pe}_xtm"], inc=sV, event=f"{pe}_bns")

        def bnag(e):
            for g in range(NG):
                e.bn_aggr(out=bn2[:, 2 * g:2 * g + 2],
                          in_=bn6[:, g * 12:g * 12 + 12])
        sch.op("vector", bnag, waits=[f"{pe}_bns"], inc=sV, event=f"{pe}_bn")

        sch.op("vector", lambda e: e.tensor_scalar_add(
                   out=bn2[:, 1:8:2], in0=bn2[:, 1:8:2], scalar1=1e-5),
               waits=[f"{pe}_bn"], inc=sV, event=f"{pe}_r0a")
        sch.op("vector", lambda e: e.reciprocal(out=bn2[:, 1:8:2],
                                                in_=bn2[:, 1:8:2]),
               waits=[f"{pe}_r0a"], inc=sV, event=f"{pe}_r0")
        sch.op("scalar", lambda e: e.activation(out=spack[:, 1:8:2],
                                                in_=bn2[:, 1:8:2], func=AF.Sqrt),
               waits=[f"{pe}_r0"], inc=sS, event=f"{pe}_r1")
        sch.op("vector", lambda e: e.tensor_mul(out=spack[:, 0:8:2],
                                                in0=bn2[:, 0:8:2],
                                                in1=spack[:, 1:8:2]),
               waits=[f"{pe}_r1"], inc=sV, event=f"{pe}_r2")
        w = [f"{pe}_r2", "l_idf"]
        if i:
            w.append(f"A{i-1}_bcmu")
        sch.op("tensor", lambda e: e.matmul(pb[5][0:8, 0:128], spack[:, 0:8],
                                            id_f[:], start=True, stop=True),
               waits=w, inc=sT, event=f"{pe}_pk")
        sch.op("vector", lambda e: e.tensor_copy(out=lrow[:],
                                                 in_=pb[5][0:8, 0:128]),
               waits=[f"{pe}_pk"], inc=sV, event=f"{pe}_lrow")

        def bc0(e):
            ins = None
            for g in range(NG):
                ins = e.matmul(pb[5][:, g * 128:(g + 1) * 128],
                               sel8[:, (2 * g) * 128:(2 * g + 1) * 128],
                               lrow[:], start=True, stop=True)
            return ins
        sch.op("tensor", bc0, waits=[f"{pe}_lrow", "l_sel"], inc=sT, event=f"{pe}_b0")

        def bc1(e):
            ins = None
            for g in range(NG):
                ins = e.matmul(pb[6][:, g * 128:(g + 1) * 128],
                               sel8[:, (2 * g + 1) * 128:(2 * g + 2) * 128],
                               lrow[:], start=True, stop=True)
            return ins
        w = [f"{pe}_lrow"]
        w.append("half_rdy" if i == 0 else f"A{i-1}_bcr")
        sch.op("tensor", bc1, waits=w, inc=sT, event=f"{pe}_b1")
        sch.op("vector", lambda e: e.tensor_copy(out=bcA_mu[:], in_=pb[5][:]),
               waits=[f"{pe}_b0"], inc=sV, event=f"{pe}_bcmu")
        sch.op("vector", lambda e: e.tensor_copy(out=bcA_r[:], in_=pb[6][:]),
               waits=[f"{pe}_b1"], inc=sV, event=f"{pe}_bcr")

        def xhat_m(e):
            ins = None
            for dc in range(NC):
                sl = slice(dc * TT, (dc + 1) * TT)
                ins = e.tensor_mul(out=x_fma[:, sl], in0=x_fma[:, sl],
                                   in1=bcA_r[:])
            return ins
        sch.op("vector", xhat_m, waits=[f"{pe}_bcmu", f"{pe}_bcr", f"{pe}_xfm"],
               inc=sV, event=f"{pe}_xhm")

        def xhat_s(e):
            ins = None
            for dc in range(NC):
                sl = slice(dc * TT, (dc + 1) * TT)
                ins = e.tensor_sub(out=x_fma[:, sl], in0=x_fma[:, sl],
                                   in1=bcA_mu[:])
            return ins
        sch.op("vector", xhat_s, waits=[f"{pe}_xhm"], inc=sV,
               event=f"{pe}_xhat")

        def h1f(e):
            ins = None
            for dc in range(NC):
                sl = slice(dc * TT, (dc + 1) * TT)
                ins = e.activation(out=h1r[i % 2][:, sl], in_=x_fma[:, sl],
                                   func=AF.Identity, scale=ca1[:, dc:dc + 1],
                                   bias=sa_c[:, dc:dc + 1])
            return ins
        w = [f"{pe}_xhat"]
        if i >= 2:
            w.append(f"A{i-2}_mmdone")
        sch.op("scalar", h1f, waits=w, inc=sS, event=f"{pe}_h1")

    def emit_mmA(i):
        pe = f"A{i}"
        hb = h1r[i % 2]
        for pi, (wt, dst, wev, fn, sc) in enumerate(
                ((wqT, q_sb, "l_wq", AF.Relu, 0.125),
                 (wuT, u_sb, "l_wu", AF.Silu, 1.0))):
            for ec in range(NC):
                bank = pb[2 + ec % 2]

                def mm(wt=wt, ec=ec, bank=bank, hb=hb):
                    def f(e):
                        ins = None
                        for dc in range(NC):
                            ins = e.matmul(
                                bank[:],
                                wt[:, dc * D + ec * 128:dc * D + ec * 128 + 128],
                                hb[:, dc * TT:(dc + 1) * TT],
                                start=(dc == 0), stop=(dc == NC - 1))
                        return ins
                    return f
                w = [f"{pe}_h1", wev]
                gi = pi * NC + ec
                if gi >= 2:
                    pj, ej = divmod(gi - 2, NC)
                    w.append(f"{pe}_pe{pj}_{ej}")
                elif i:
                    w.append(f"A{i-1}_kve{NG-1}_{2 + ec}")
                sch.op("tensor", mm(), waits=w, inc=sT, event=f"{pe}_p{pi}_{ec}")

                def ev(ec=ec, bank=bank, dst=dst, fn=fn, sc=sc, i=i):
                    def f(e):
                        return e.activation(
                            out=dst[:, ec * T + i * TT:ec * T + i * TT + TT],
                            in_=bank[:], func=fn, scale=sc)
                    return f
                sch.op("scalar", ev(), waits=[f"{pe}_p{pi}_{ec}"], inc=sS,
                       event=f"{pe}_pe{pi}_{ec}")
        for g in range(NG):
            for pi, (wt, dst, wev, fn, sc) in enumerate(
                    ((wkT, kTM, "l_wk", AF.Relu, 0.125),
                     (wvT, vTM, "l_wv", AF.Identity, 1.0))):
                for hf in range(2):
                    idx = g * 4 + pi * 2 + hf
                    bank = pb[2 + idx % 2]

                    def mm(wt=wt, g=g, hf=hf, bank=bank, hb=hb):
                        def f(e):
                            ins = None
                            for dc in range(NC):
                                ins = e.matmul(
                                    bank[:],
                                    hb[:, dc * TT + g * 128:dc * TT + g * 128 + 128],
                                    wt[:, dc * D + hf * 512:dc * D + (hf + 1) * 512],
                                    start=(dc == 0), stop=(dc == NC - 1))
                            return ins
                        return f
                    w = [f"{pe}_h1", wev]
                    if idx >= 2:
                        pg, r = divmod(idx - 2, 4)
                        w.append(f"{pe}_kve{pg}_{r}")
                    else:
                        w.append(f"{pe}_pe1_{NC - 2 + idx}")
                    if pi == 0 and hf == 0 and g:
                        w.append(f"{pe}_kv{g-1}")
                    sch.op("tensor", mm(), waits=w, inc=sT,
                           event=f"{pe}_kvm{g}_{pi * 2 + hf}")

                    def ev(hf=hf, bank=bank, dst=dst, fn=fn, sc=sc):
                        def f(e):
                            return e.activation(out=dst[:, hf * 512:(hf + 1) * 512],
                                                in_=bank[:], func=fn, scale=sc)
                        return f
                    sch.op("scalar", ev(), waits=[f"{pe}_kvm{g}_{pi * 2 + hf}"],
                           inc=sS, event=f"{pe}_kve{g}_{pi * 2 + hf}")

            def kv(g=g, i=i):
                def f(e):
                    ins = None
                    for p in range(NP):
                        bank = pb[p // 4]
                        ins = e.matmul(bank[:, (p % 4) * 128:(p % 4 + 1) * 128],
                                       kTM[:, p * 128:(p + 1) * 128],
                                       vTM[:, p * 128:(p + 1) * 128],
                                       start=(i == 0 and g == 0 and p % 4 == 0),
                                       stop=(i == NT - 1 and g == NG - 1 and p % 4 == 3))
                    return ins
                return f
            sch.op("tensor", kv(), waits=[f"{pe}_kve{g}_1", f"{pe}_kve{g}_3"],
                   inc=sT, event=f"{pe}_kv{g}")
        sch.op("tensor", lambda e: None,
               waits=[f"{pe}_kv{NG-1}", f"{pe}_p1_{NC-1}"],
               inc=sT, event=f"{pe}_mmdone")

    emit_prepA(0)
    for i in range(NT):
        if i + 1 < NT:
            emit_prepA(i + 1)
        emit_mmA(i)

    A_END = f"A{NT-1}_mmdone"
    sch.op("sync", ld(fwS[:], dap(dws["ffn_w"], 0, [[D, 128], [128 * D, NC], [1, D]])),
           waits=[A_END], inc=sFW, event="l_fw", inc_n=16)
    sch.op("gpsimd", wfull(woT, "wo_t"), waits=[A_END], inc=sWo,
           event="l_wo", inc_n=16)

    def famax(e):
        ins = None
        for ec in range(NC):
            ins = e.tensor_reduce(out=ws_c[:, ec:ec + 1],
                                  in_=fwS[:, ec * D:(ec + 1) * D],
                                  axis=AX, op=ALU.max,
                                  apply_absolute_value=True)
        return ins
    sch.op("vector", famax, waits=["l_fw"], inc=sV, event="f_amax")
    sch.op("gpsimd", lambda e: e.tensor_scalar_max(out=ws2[:], in0=ws_c[:],
                                                   scalar1=1e-8),
           waits=["f_amax"], inc=sP, event="f_clamp")

    sch.op("scalar", lambda e: e.activation(out=wrs_c[:], in_=ws2[:],
                                            func=AF.Ln, scale=1.0 / 127.0),
           waits=["f_clamp"], inc=sS, event="f_rsa")

    def f_rs(e):
        e.activation(out=wrs_c[:], in_=wrs_c[:], func=AF.Exp, scale=-1.0)
        return e.activation(out=ws_c[:], in_=ws2[:], func=AF.Identity,
                            scale=1.0 / 127.0)
    sch.op("scalar", f_rs, waits=["f_rsa"], inc=sS, event="f_rs")

    def fscale(e):
        e.tensor_mul(out=gws_c[:], in0=gb_c[:], in1=ws_c[:])
        return e.tensor_mul(out=gbb_c[:], in0=gb_c[:], in1=bias_c[:])
    sch.op("vector", fscale, waits=["f_rs", "mod_rdy", "l_bias"], inc=sV,
           event="f_scale")
    for ec in range(NC):
        def sgn1(ec=ec):
            def f(e):
                return e.activation(out=wsg1[:], in_=fwS[:, ec * D:(ec + 1) * D],
                                    func=AF.Sign)
            return f
        w = ["l_fw", A_END]
        if ec:
            w.append(f"f_wqv{ec - 1}")
        sch.op("scalar", sgn1(), waits=w, inc=sS, event=f"f_sg{ec}")

        def wq1a(ec=ec):
            def f(e):
                sl = slice(ec * D, (ec + 1) * D)
                return e.tensor_scalar(out=fwS[:, sl], in0=fwS[:, sl],
                                       scalar1=wrs_c[:, ec:ec + 1],
                                       scalar2=None, op0=ALU.mult)
            return f
        w = ["f_rs", f"f_sg{ec}"]
        if ec:
            w.append(f"f_wq{ec - 1}")
        sch.op("vector", wq1a(), waits=w, inc=sV, event=f"f_wqa{ec}")

        def wq1b(ec=ec):
            def f(e):
                return e.scalar_tensor_tensor(out=wqi[:], in0=wsg1[:],
                                              scalar=half_c[:, 0:1],
                                              in1=fwS[:, ec * D:(ec + 1) * D],
                                              op0=ALU.mult, op1=ALU.add)
            return f
        sch.op("vector", wq1b(), waits=[f"f_wqa{ec}", "half_rdy"],
               inc=sV, event=f"f_wqv{ec}")

        def wq2(ec=ec):
            def f(e):
                return e.copy(out=wqb[:, ec * D:(ec + 1) * D], in_=wqi[:])
            return f
        sch.op("scalar", wq2(), waits=[f"f_wqv{ec}"], inc=sS, event=f"f_wq{ec}")
    _tq_idxs = []
    for dcb in range(NC):
        for hf in range(2):
            def tq(dcb=dcb, hf=hf):
                def f(e):
                    ins = None
                    for eq in range(4):
                        ec = hf * 4 + eq
                        ins = e.matmul(
                            pb[7][:, eq * 128:(eq + 1) * 128],
                            wqb[:, ec * D + dcb * 128:ec * D + dcb * 128 + 128],
                            id_b[:], start=True, stop=True)
                    return ins
                return f
            w = [f"f_wq{hf * 4 + 3}", A_END]
            ti = dcb * 2 + hf
            if ti:
                w.append(f"f_te{(ti - 1) // 2}_{(ti - 1) % 2}")
            _tq_idxs.append(len(sch.ops))
            sch.op("tensor", tq(), waits=w, inc=sT, event=f"f_t{dcb}_{hf}")

            def tqe(dcb=dcb, hf=hf):
                def f(e):
                    return e.copy(
                        out=fqT[:, dcb * D + hf * 512:dcb * D + (hf + 1) * 512],
                        in_=pb[7][:])
                return f
            _tq_idxs.append(len(sch.ops))
            sch.op("scalar", tqe(), waits=[f"f_t{dcb}_{hf}"], inc=sS,
                   event=f"f_te{dcb}_{hf}")
    F_END = f"f_te{NC-1}_1"

    sch.op("gpsimd", lambda e: e.memset(kv_sb[:], 0.0), inc=sP, event="kvz")
    sch.op("gpsimd", lambda e: e.memset(rowsB[:], 0.0), waits=[A_END],
           inc=sP, event="rbz")

    def kvev(e):
        ins = None
        for p in range(NP):
            bank = pb[p // 4]
            c0 = (p % 4) * 128
            ins = e.tensor_copy(out=kv_sb[0:64, p * 128:p * 128 + 64],
                                in_=bank[0:64, c0:c0 + 64])
            ins = e.tensor_copy(out=kv_sb[64:128, p * 128 + 64:(p + 1) * 128],
                                in_=bank[64:128, c0 + 64:c0 + 128])
        return ins
    sch.op("vector", kvev, waits=[A_END, "kvz"], inc=sV, event="kv_rdy")

    _P_B0 = float(len(sch.ops))
    _opath = []
    _lnm_idx = []
    for i in range(NT):
        pe = f"B{i}"
        _opath.append([])
        for p in range(NP):
            def om(p=p, i=i):
                def f(e):
                    return e.matmul(pb[p % 2][:], kv_sb[:, p * 128:(p + 1) * 128],
                                    q_sb[:, p * T + i * TT:p * T + i * TT + TT],
                                    start=True, stop=True)
                return f
            w = ["kv_rdy"]
            if p >= 2:
                w.append(f"{pe}_oe{p - 2}")
            elif i:
                w += [f"B{i-1}_oe{NP - 2 + p}", f"B{i-1}_ou{NP - 1}"]
            _opath[i].append(len(sch.ops))
            sch.op("tensor", om(), waits=w, inc=sT, event=f"{pe}_o{p}")

            def oev(p=p):
                def f(e):
                    e.activation(out=o_bf[:, p * TT:(p + 1) * TT],
                                 in_=pb[p % 2][:], func=AF.Identity)
                    return e.activation(out=osq[:], in_=pb[p % 2][:], func=AF.Square)
                return f
            w = [f"{pe}_o{p}"]
            if p:
                w.append(f"{pe}_ss{p - 1}")
            if p == 0 and i == 0:
                w.append(A_END)
            if p == 0 and i:
                w.append(f"B{i-1}_l2m")
            _opath[i].append(len(sch.ops))
            sch.op("scalar", oev(), waits=w, inc=sS, event=f"{pe}_oe{p}")

            def ssm(p=p):
                def f(e):
                    return e.matmul(pb[2][0:16, :], mss[:, p * 16:(p + 1) * 16],
                                    osq[:], start=(p == 0), stop=(p == NP - 1))
                return f
            w = [f"{pe}_oe{p}", "l_mss"]
            if p == 0 and i:
                w.append(f"B{i-1}_sse")
            if p == 0 and i == 0:
                w.append(f"A{NT-1}_kve{NG-1}_2")
            _opath[i].append(len(sch.ops))
            sch.op("tensor", ssm(), waits=w, inc=sT, event=f"{pe}_ss{p}")
        w = [f"{pe}_ss{NP-1}", "rbz"]
        if i:
            w.append(f"B{i-1}_lb1")
            w.append(f"B{i-1}_lb0")
        sch.op("vector", lambda e, w=w: e.tensor_copy(out=ss_s[0:16, :], in_=pb[2][0:16, :]),
               waits=w, inc=sV, event=f"{pe}_sse")
        sch.op("scalar", lambda e: e.activation(out=ss_s[0:16, :], in_=ss_s[0:16, :], func=AF.Sqrt),
               waits=[f"{pe}_sse"], inc=sS, event=f"{pe}_sq")

        sch.op("vector", lambda e: e.tensor_scalar_max(
                   out=ss_s[0:16, :], in0=ss_s[0:16, :], scalar1=8e-12),
               waits=[f"{pe}_sq"], inc=sV, event=f"{pe}_sr0")
        sch.op("vector", lambda e: e.reciprocal(out=ss_s[0:16, :],
                                                in_=ss_s[0:16, :]),
               waits=[f"{pe}_sr0"], inc=sV, event=f"{pe}_sr1")
        sch.op("vector", lambda e: e.tensor_scalar(
                   out=r16[:], in0=ss_s[0:16, :], scalar1=8.0,
                   scalar2=None, op0=ALU.mult),
               waits=[f"{pe}_sr1"], inc=sV, event=f"{pe}_r16")
        for p in range(NP):
            bank = pb[3 if p % 2 == 0 else 6]

            def rbm(p=p, bank=bank):
                def f(e):
                    return e.matmul(bank[:], mbc[:, p * 128:(p + 1) * 128],
                                    r16[:], start=True, stop=True)
                return f
            w = [f"{pe}_r16", "l_mbc"]
            if p >= 2:
                w.append(f"{pe}_ou{p - 2}")
            elif i:
                w.append(f"B{i-1}_xqv{NC - 1}" if p == 0 else f"B{i-1}_xse")
            sch.op("tensor", rbm(), waits=w, inc=sT, event=f"{pe}_rb{p}")

            def ouf1(p=p, bank=bank):
                def f(e):
                    return e.tensor_mul(out=ouxq[:, p * TT:(p + 1) * TT],
                                        in0=o_bf[:, p * TT:(p + 1) * TT],
                                        in1=bank[:])
                return f
            w = [f"{pe}_rb{p}", f"{pe}_oe{p}"]
            if p == 0 and i > 0:
                w.append(f"B{i-1}_xqc{NC - 1}")
            sch.op("vector", ouf1(), waits=w, inc=sV, event=f"{pe}_oa{p}")

            def ouf2(p=p, i=i):
                def f(e):
                    return e.tensor_mul(
                        out=ouxq[:, p * TT:(p + 1) * TT],
                        in0=ouxq[:, p * TT:(p + 1) * TT],
                        in1=u_sb[:, p * T + i * TT:p * T + i * TT + TT])
                return f
            sch.op("vector", ouf2(), waits=[f"{pe}_oa{p}"], inc=sV,
                   event=f"{pe}_ou{p}")
        for ec in range(NC):
            bank = pb[4 + ec % 2]

            def wom(ec=ec, bank=bank):
                def f(e):
                    ins = None
                    for dc in range(NC):
                        ins = e.matmul(
                            bank[:],
                            woT[:, dc * D + ec * 128:dc * D + ec * 128 + 128],
                            ouxq[:, dc * TT:(dc + 1) * TT],
                            start=(dc == 0), stop=(dc == NC - 1))
                    return ins
                return f
            w = [f"{pe}_ou{NP - 1}", "l_wo"]
            if ec >= 2:
                w.append(f"{pe}_x2c{ec - 2}")
            elif i:
                w.append(f"B{i-1}_ff{NC - 2 + ec}")
            elif ec != 0:
                w.append(f"A{NT-1}_bcmu")
            sch.op("tensor", wom(), waits=w, inc=sT, event=f"{pe}_wo{ec}")

            def x2f(ec=ec, bank=bank):
                def f(e):
                    return e.tensor_scalar(out=x2[:, ec * TT:(ec + 1) * TT],
                                           in0=bank[:], scalar1=ga_c[:, ec:ec + 1],
                                           scalar2=None, op0=ALU.mult)
                return f
            w = [f"{pe}_wo{ec}"]
            if ec == 0:
                w.append(F_END if i == 0 else f"B{i-1}_x3{NC-1}")
            sch.op("vector", x2f(), waits=w, inc=sV, event=f"{pe}_x2c{ec}")
        for hf in range(2):
            w = []
            if i == 0 and hf == 0:
                w.append(f"f_wq{NC - 1}")
            if i or hf:
                w.append(f"{pe}_xa{hf-1}" if hf else f"B{i-1}_xa1")
            if hf == 0 and i:
                w.append(f"B{i-1}_xa1")
            sch.op("sync", ld(x_fmb[:], dap(dx_fm, i * TT + hf * 256,
                                            [[T, 128], [128 * T, NC], [1, 256]])),
                   waits=w, inc=sXB, event=f"{pe}_xf{hf}", inc_n=16)

            def xadd(hf=hf):
                def f(e):
                    ins = None
                    for ec in range(NC):
                        sl = slice(ec * TT + hf * 256, ec * TT + hf * 256 + 256)
                        ins = e.tensor_add(out=x2[:, sl], in0=x2[:, sl],
                                           in1=x_fmb[:, ec * 256:(ec + 1) * 256])
                    return ins
                return f
            sch.op("vector", xadd(), waits=[f"{pe}_xf{hf}", f"{pe}_x2c{NC-1}"],
                   inc=sV, event=f"{pe}_xa{hf}")
        def x2bf(e):
            return e.activation(out=ouxq[:], in_=x2[:], func=AF.Identity)
        sch.op("scalar", x2bf, waits=[f"{pe}_xa1", f"{pe}_wo{NC-1}"],
               inc=sS, event=f"{pe}_x2b")

        def x2sq(e):
            return e.tensor_mul(out=o_bf[:], in0=x2[:], in1=x2[:])
        sch.op("vector", x2sq, waits=[f"{pe}_xa1", f"{pe}_ou{NP-1}"],
               inc=sV, event=f"{pe}_x2q")

        def lnm(e):
            ins = None
            for dc in range(NC):
                ins = e.matmul(pb[7][0:1, :], zc_b[:, 0:1],
                               ouxq[:, dc * TT:(dc + 1) * TT],
                               start=(dc == 0), stop=(dc == NC - 1))
            for dc in range(NC):
                ins = e.matmul(pb[6][0:1, :], zc_b[:, 0:1],
                               o_bf[:, dc * TT:(dc + 1) * TT],
                               start=(dc == 0), stop=(dc == NC - 1))
            return ins
        _lnm_idx.append(len(sch.ops))
        w = [f"{pe}_x2b", f"{pe}_x2q", "l_zcb", f"{pe}_ss{NP-1}",
             f"{pe}_ou{NP-1}"]
        if i:
            w.append(f"B{i-1}_qr")
        else:
            w.append(F_END)
        sch.op("tensor", lnm, waits=w, inc=sT, event=f"{pe}_l2m")

        def l2ev(e):
            e.tensor_copy(out=rowsB[0:1, :], in_=pb[7][0:1, :])
            return e.tensor_copy(out=rowsB[32:33, :], in_=pb[6][0:1, :])
        sch.op("vector", l2ev, waits=[f"{pe}_l2m"], inc=sV, event=f"{pe}_l2ev")

        def l2g(e):
            ins = None
            for g in range(NG):
                ins = e.matmul(pb[7][:, 8 + g * 2:8 + g * 2 + 2],
                               rowsB[0:33, g * 128:(g + 1) * 128], selr[:],
                               start=True, stop=True)
            return ins
        sch.op("tensor", l2g, waits=[f"{pe}_l2ev", "l_selr"], inc=sT,
               event=f"{pe}_l2g")

        sch.op("vector", lambda e: e.tensor_scalar_mul(
                   out=bn2[:], in0=pb[7][:, 8:16], scalar1=1.0 / D),
               waits=[f"{pe}_l2g"], inc=sV, event=f"{pe}_l2ca")
        sch.op("vector", lambda e: e.tensor_mul(
                   out=spack[:, 0:8:2], in0=bn2[:, 0:8:2], in1=bn2[:, 0:8:2]),
               waits=[f"{pe}_l2ca"], inc=sV, event=f"{pe}_l2cb")
        sch.op("vector", lambda e: e.tensor_sub(
                   out=bn2[:, 1:8:2], in0=bn2[:, 1:8:2], in1=spack[:, 0:8:2]),
               waits=[f"{pe}_l2cb"], inc=sV, event=f"{pe}_l2cc")
        sch.op("vector", lambda e: e.tensor_scalar_add(
                   out=bn2[:, 1:8:2], in0=bn2[:, 1:8:2], scalar1=1e-5),
               waits=[f"{pe}_l2cc"], inc=sV, event=f"{pe}_l2cd")
        sch.op("vector", lambda e: e.reciprocal(out=bn2[:, 1:8:2],
                                                in_=bn2[:, 1:8:2]),
               waits=[f"{pe}_l2cd"], inc=sV, event=f"{pe}_l2c")
        sch.op("scalar", lambda e: e.activation(out=spack[:, 1:8:2],
                                                in_=bn2[:, 1:8:2], func=AF.Sqrt),
               waits=[f"{pe}_l2c"], inc=sS, event=f"{pe}_l2s")
        sch.op("vector", lambda e: e.tensor_mul(out=spack[:, 0:8:2],
                                                in0=bn2[:, 0:8:2],
                                                in1=spack[:, 1:8:2]),
               waits=[f"{pe}_l2s"], inc=sV, event=f"{pe}_l2c2")
        sch.op("tensor", lambda e: e.matmul(pb[7][0:8, 0:128], spack[:, 0:8],
                                            id_f[:], start=True, stop=True),
               waits=[f"{pe}_l2c2"], inc=sT, event=f"{pe}_l2pk")
        sch.op("vector", lambda e: e.tensor_copy(out=lrow[:],
                                                 in_=pb[7][0:8, 0:128]),
               waits=[f"{pe}_l2pk"], inc=sV, event=f"{pe}_l2e")
        def l2b0(e):
            ins = None
            for g in range(NG):
                ins = e.matmul(pb[3][:, g * 128:(g + 1) * 128],
                               sel8[:, (2 * g + 1) * 128:(2 * g + 2) * 128],
                               lrow[:], start=True, stop=True)
            return ins
        sch.op("tensor", l2b0, waits=[f"{pe}_l2e", f"{pe}_ou{NP-1}", "l_selr"],
               inc=sT, event=f"{pe}_lb0")

        def l2b1(e):
            ins = None
            for g in range(NG):
                ins = e.matmul(pb[6][:, g * 128:(g + 1) * 128],
                               sel8[:, (2 * g) * 128:(2 * g + 1) * 128],
                               lrow[:], start=True, stop=True)
            return ins
        sch.op("tensor", l2b1, waits=[f"{pe}_l2e", f"{pe}_ou{NP-1}"], inc=sT,
               event=f"{pe}_lb1")
        def xh2m(e):
            for dc in range(NC):
                sl = slice(dc * TT, (dc + 1) * TT)
                e.tensor_mul(out=h2[:, sl], in0=x2[:, sl], in1=pb[3][:])
        w = [f"{pe}_lb0", f"{pe}_lb1", f"{pe}_xa1"]
        w.append(F_END if i == 0 else f"B{i-1}_outd")
        sch.op("vector", xh2m, waits=w, inc=sV, event=f"{pe}_xh2m")

        def xh2s(e):
            for dc in range(NC):
                sl = slice(dc * TT, (dc + 1) * TT)
                e.tensor_sub(out=h2[:, sl], in0=h2[:, sl], in1=pb[6][:])
        sch.op("vector", xh2s, waits=[f"{pe}_xh2m"], inc=sV, event=f"{pe}_xh2")

        def h2f(e):
            ins = None
            for dc in range(NC):
                sl = slice(dc * TT, (dc + 1) * TT)
                ins = e.activation(out=h2[:, sl], in_=h2[:, sl], func=AF.Identity,
                                   scale=cb1[:, dc:dc + 1], bias=sb_c[:, dc:dc + 1])
            return ins
        sch.op("scalar", h2f, waits=[f"{pe}_xh2"], inc=sS, event=f"{pe}_h2")
        def hamax(e):
            return e.tensor_reduce(
                out=M_sb[:, 0:TT],
                in_=h2[:].rearrange("p (c t) -> p t c", c=NC),
                axis=AX, op=ALU.max, apply_absolute_value=True)
        sch.op("vector", hamax, waits=[f"{pe}_h2", f"{pe}_xa1"], inc=sV,
               event=f"{pe}_amx")
        def mtr(e):
            ins = None
            for g in range(NG):
                ins = e.matmul(pb[7][:, g * 128:(g + 1) * 128],
                               M_sb[:, g * 128:(g + 1) * 128], id_b[:],
                               start=True, stop=True)
            return ins
        sch.op("tensor", mtr, waits=[f"{pe}_amx", f"{pe}_l2e"], inc=sT,
               event=f"{pe}_mtr")

        def ared(e):
            ins = None
            for g in range(NG):
                ins = e.tensor_reduce(out=amx[:, g:g + 1],
                                      in_=pb[7][:, g * 128:(g + 1) * 128],
                                      axis=AX, op=ALU.max)
            return ins
        w = [f"{pe}_mtr"]
        if i:
            w.append(f"B{i-1}_am2")
        sch.op("vector", ared, waits=w, inc=sV, event=f"{pe}_am")
        w = [f"{pe}_am"]
        if i:
            w.append(f"B{i-1}_qs")
        sch.op("gpsimd", lambda e: e.tensor_scalar_max(out=amx2[:], in0=amx[:],
                                                       scalar1=1e-8),
               waits=w, inc=sP, event=f"{pe}_am2")

        w = [f"{pe}_am2"]
        if i:
            w.append(f"B{i-1}_qt")
        sch.op("scalar", lambda e: e.activation(out=qpack[:, 0:4],
                                                in_=amx2[:], func=AF.Ln,
                                                scale=1.0 / 127.0),
               waits=w, inc=sS, event=f"{pe}_qsa")

        def qsc(e):
            e.activation(out=qpack[:, 0:4], in_=qpack[:, 0:4], func=AF.Exp,
                         scale=-1.0)
            return e.activation(out=qpack[:, 4:8], in_=amx2[:],
                                func=AF.Identity, scale=1.0 / 127.0)
        sch.op("scalar", qsc, waits=[f"{pe}_qsa"], inc=sS, event=f"{pe}_qs")
        sch.op("tensor", lambda e: e.matmul(pb[7][0:8, 0:128], qpack[:, 0:8],
                                            id_f[:], start=True, stop=True),
               waits=[f"{pe}_qs"], inc=sT, event=f"{pe}_qt")
        sch.op("scalar", lambda e: e.copy(out=qrow[:], in_=pb[7][0:8, 0:128]),
               waits=[f"{pe}_qt"], inc=sS, event=f"{pe}_qr")
        def qb0(e):
            ins = None
            for g in range(NG):
                ins = e.matmul(pb[3][:, g * 128:(g + 1) * 128],
                               sel8[:, g * 128:(g + 1) * 128],
                               qrow[:], start=True, stop=True)
            return ins
        sch.op("tensor", qb0, waits=[f"{pe}_qr", f"{pe}_xh2"], inc=sT,
               event=f"{pe}_qb0")

        def qb1(e):
            ins = None
            for g in range(NG):
                ins = e.matmul(pb[6][:, g * 128:(g + 1) * 128],
                               sel8[:, (4 + g) * 128:(5 + g) * 128],
                               qrow[:], start=True, stop=True)
            return ins
        sch.op("tensor", qb1, waits=[f"{pe}_qr", f"{pe}_xh2"], inc=sT,
               event=f"{pe}_qb1")
        w = [f"{pe}_qb1"]
        if i == 0:
            w.append(F_END)
        else:
            w.append(f"B{i-1}_ffe{NC-1}")
        sch.op("scalar", lambda e: e.copy(out=xs_sb[:], in_=pb[6][:]),
               waits=w, inc=sS, event=f"{pe}_xse")
        for dc in range(NC):
            def xqv(dc=dc):
                def f(e):
                    return e.tensor_mul(out=h2[:, dc * TT:(dc + 1) * TT],
                                        in0=h2[:, dc * TT:(dc + 1) * TT], in1=pb[3][:])
                return f
            w = [f"{pe}_qb0", f"{pe}_h2"]
            sch.op("vector", xqv(), waits=w, inc=sV, event=f"{pe}_xqv{dc}")

            def xqs(dc=dc):
                def f(e):
                    return e.activation(out=sgn2[:],
                                        in_=h2[:, dc * TT:(dc + 1) * TT],
                                        func=AF.Sign)
                return f
            w = [f"{pe}_xqv{dc}"]
            if dc:
                w.append(f"{pe}_xqi{dc - 1}")
            sch.op("scalar", xqs(), waits=w, inc=sS, event=f"{pe}_xqs{dc}")

            def xqr(dc=dc):
                def f(e):
                    return e.scalar_tensor_tensor(
                        out=xq_i[:], in0=sgn2[:], scalar=half_c[:, 0:1],
                        in1=h2[:, dc * TT:(dc + 1) * TT],
                        op0=ALU.mult, op1=ALU.add)
                return f
            w = [f"{pe}_xqs{dc}"]
            if dc:
                w.append(f"{pe}_xqc{dc - 1}")
            sch.op("vector", xqr(), waits=w, inc=sV, event=f"{pe}_xqi{dc}")

            def xqc2(dc=dc):
                def f(e):
                    return e.copy(out=ouxq[:, dc * TT:(dc + 1) * TT],
                                  in_=xq_i[:])
                return f
            w = [f"{pe}_xqi{dc}", f"{pe}_l2m"]
            sch.op("scalar", xqc2(), waits=w, inc=sS, event=f"{pe}_xqc{dc}")
        def x2p(e):
            ins = None
            for ec in range(NC):
                ins = e.activation(out=x2[:, ec * TT:(ec + 1) * TT],
                                   in_=x2[:, ec * TT:(ec + 1) * TT],
                                   func=AF.Identity, bias=gbb_c[:, ec:ec + 1])
            return ins
        sch.op("scalar", x2p, waits=[f"{pe}_xh2", f"{pe}_x2q", "f_scale"], inc=sS,
               event=f"{pe}_x2p")
        for ec in range(NC):
            bank = pb[4 + ec % 2]

            def ffm(ec=ec, bank=bank):
                def f(e):
                    ins = None
                    for dc in range(NC):
                        ins = e.matmul(
                            bank[:],
                            fqT[:, dc * D + ec * 128:dc * D + ec * 128 + 128],
                            ouxq[:, dc * TT:(dc + 1) * TT],
                            start=(dc == 0), stop=(dc == NC - 1))
                    return ins
                return f
            w = [f"{pe}_xqc{NC-1}"]
            if ec >= 2:
                w.append(f"{pe}_ffe{ec - 2}")
            else:
                w.append(f"{pe}_x2c{NC - 1}")
            if i == 0:
                w.append(F_END)
            sch.op("tensor", ffm(), waits=w, inc=sT, event=f"{pe}_ff{ec}")

            def ffev(ec=ec, bank=bank):
                def f(e):
                    return e.scalar_tensor_tensor(
                        out=h2[:, ec * TT:(ec + 1) * TT], in0=bank[:],
                        scalar=gws_c[:, ec:ec + 1], in1=xs_sb[:],
                        op0=ALU.mult, op1=ALU.mult)
                return f
            sch.op("vector", ffev(), waits=[f"{pe}_ff{ec}", f"{pe}_xse", "f_scale",
                                            f"{pe}_xqv{ec}"],
                   inc=sV, event=f"{pe}_ffe{ec}")

            def x3f(ec=ec):
                def f(e):
                    return e.tensor_add(out=h2[:, ec * TT:(ec + 1) * TT],
                                        in0=h2[:, ec * TT:(ec + 1) * TT],
                                        in1=x2[:, ec * TT:(ec + 1) * TT])
                return f
            sch.op("gpsimd", x3f(), waits=[f"{pe}_ffe{ec}", f"{pe}_x2p"],
                   inc=sP, event=f"{pe}_x3{ec}")
        def store(i=i):
            def f(e):
                return e.dma_start(
                    out=dap(dout, i * TT, [[NT * TT, 128], [128 * NT * TT, NC], [1, TT]]),
                    in_=h2[:])
            return f
        sch.op("sync", store(), waits=[f"{pe}_x3{ec2}" for ec2 in range(NC)],
               inc=sO, event=f"{pe}_out", inc_n=16)
        sch.op("sync", lambda e: None, waits=[f"{pe}_out"],
               inc=sD, event=f"{pe}_outd")

    for _j, _oi in enumerate(_tq_idxs):
        sch.ops[_oi][0] = _P_B0 + 23.5 + _j * 0.001
    for _i in range(1, NT):
        for _j, _oi in enumerate(_opath[_i]):
            sch.ops[_oi][0] = _lnm_idx[_i - 1] + 0.5 + _j * 0.001

    sch.op("sync", lambda e: None, waits=[f"B{NT-1}_outd"], inc=sD,
           event="all_done")

    with nc.Block() as block:
        @block.sync
        def _(eng):
            sch.emit_engine("sync", eng)

        @block.gpsimd
        def _(eng):
            sch.emit_engine("gpsimd", eng)

        @block.vector
        def _(eng):
            sch.emit_engine("vector", eng)

        @block.scalar
        def _(eng):
            sch.emit_engine("scalar", eng)

        @block.tensor
        def _(eng):
            sch.emit_engine("tensor", eng)

    return nc


def _host_aux():
    ident = np.eye(128)
    msk_ss = np.zeros((NP, 128, 16), dtype=np.float32)
    msk_bc = np.zeros((NP, 16, 128), dtype=np.float32)
    for p in range(NP):
        for r in range(128):
            h = 2 * p + (1 if r >= 64 else 0)
            msk_ss[p, r, h] = 1.0
            msk_bc[p, h, r] = 1.0
    zc = np.zeros((128, 3), dtype=np.float32)
    zc[:, 0] = 1.0
    zc[:, 2] = 1.0
    import ml_dtypes
    bf = ml_dtypes.bfloat16
    return {
        "ident_bf": ident.astype(bf),
        "ident_f": ident.astype(np.float32),
        "ones_f": np.ones((1, 128), dtype=np.float32),
        "zcols_b": zc.astype(bf),
        "msk_ss": msk_ss.astype(bf),
        "msk_bc": msk_bc.astype(bf),
        "sel8": np.concatenate(
            [np.outer((np.arange(8) == r).astype(np.float32),
                      np.ones(128, np.float32)) for r in range(8)],
            axis=1).astype(bf),
        "sel33b": np.stack([(np.arange(33) == 0).astype(np.float32),
                            (np.arange(33) == 32).astype(np.float32)], axis=1),
    }


def kernel(L0_attn_x, L0_attn_Wq, L0_attn_Wk, L0_attn_Wv, L0_attn_Wu,
           L0_attn_Wo, L0_ffn_weight, L0_ffn_bias, L0_c, L0_wmod):
    x = np.ascontiguousarray(np.asarray(L0_attn_x, dtype=np.float32))
    aux = _host_aux()
    shared = {
        "wq_t": np.ascontiguousarray(np.asarray(L0_attn_Wq, np.float32).T),
        "wk_t": np.ascontiguousarray(np.asarray(L0_attn_Wk, np.float32).T),
        "wv_t": np.ascontiguousarray(np.asarray(L0_attn_Wv, np.float32).T),
        "wu_t": np.ascontiguousarray(np.asarray(L0_attn_Wu, np.float32).T),
        "wo_t": np.ascontiguousarray(np.asarray(L0_attn_Wo, np.float32).T),
        "ffn_w": np.ascontiguousarray(np.asarray(L0_ffn_weight, np.float32)),
        "ffn_b": np.ascontiguousarray(
            np.asarray(L0_ffn_bias, np.float32).reshape(8, 128).T),
        "wmod": np.ascontiguousarray(np.asarray(L0_wmod, np.float32)),
        **aux,
    }
    carr = np.asarray(L0_c, np.float32)
    in_maps = []
    for b in range(8):
        m = dict(shared)
        m["x_fm"] = np.ascontiguousarray(x[b].T)
        m["x_tm"] = np.ascontiguousarray(x[b])
        m["c_vec"] = np.ascontiguousarray(carr[b].reshape(8, 128).T)
        in_maps.append(m)
    nc = build_nc()
    res = run_bass_kernel_spmd(nc, in_maps, core_ids=list(range(8)))
    out = np.empty((8, T, D), dtype=np.float32)
    for b in range(8):
        o = np.asarray(res.results[b]["out_fm"])
        out[b] = o.reshape(D, NT * TT).T
    return out


# version 16 (final)
# speedup vs baseline: 1.0200x; 1.0200x over previous
import numpy as np
import concourse.bass as bass
import concourse.mybir as mybir
from concourse.bass_utils import run_bass_kernel_spmd

F32 = mybir.dt.float32
BF16 = mybir.dt.bfloat16
I32 = mybir.dt.int32
AF = mybir.ActivationFunctionType
ALU = mybir.AluOpType
AX = mybir.AxisListType.X

D = 1024
T = 2048
NT = 4
TT = 512
NG = 4
NC = 8
NP = 8
MAGIC = 12582912.0
PROBE = 0.3


class _MockIns:
    def then_inc(self, *a, **kw):
        return self


class _CountEng:

    def __init__(self):
        self.n = 0

    def wait_ge(self, *a, **kw):
        pass

    def sem_inc(self, *a, **kw):
        pass

    def __getattr__(self, name):
        def f(*a, **kw):
            self.n += 1
            return _MockIns()
        return f


class _IncEng:

    def __init__(self, eng, sem, inc_n):
        self._eng = eng
        self._sem = sem
        self._inc_n = inc_n
        self.n = 0

    def wait_ge(self, *a, **kw):
        return self._eng.wait_ge(*a, **kw)

    def sem_inc(self, *a, **kw):
        return self._eng.sem_inc(*a, **kw)

    def __getattr__(self, name):
        f = getattr(self._eng, name)
        if not callable(f):
            return f

        def wrap(*a, **kw):
            ins = f(*a, **kw)
            if ins is not None and hasattr(ins, "then_inc"):
                ins.then_inc(self._sem, self._inc_n)
                self.n += 1
            return ins
        return wrap


class Sched:

    def __init__(self, nc):
        self.nc = nc
        self.ops = []
        self.evset = set()
        self.alias = {}
        self.sorted_ops = None

    def sem(self, name):
        return self.nc.alloc_semaphore(name)

    def op(self, engine, emit, waits=(), inc=None, event=None, inc_n=1,
           key=None):
        for w in waits:
            assert w in self.evset, f"unknown event {w}"
        mock = _CountEng()
        emit(mock)
        n_ins = mock.n
        if event is not None:
            assert inc is not None
            assert event not in self.evset, f"dup event {event}"
            self.evset.add(event)
        if key is None:
            key = float(len(self.ops))
        self.ops.append([key, engine, emit, tuple(waits), inc, inc_n,
                         n_ins, event])

    def alias_event(self, name, target):
        assert target in self.evset and name in self.evset
        self.alias[name] = target

    def finalize(self):
        self.sorted_ops = sorted(self.ops, key=lambda t: t[0])
        counts = {}
        self.evval = {}
        for key, eng, emit, wn, inc, inc_n, n_ins, event in self.sorted_ops:
            if inc is not None:
                c = counts.get(inc.name, 0) + inc_n * max(n_ins, 1)
                counts[inc.name] = c
                if event is not None:
                    self.evval[event] = (inc, c)
        for name, target in self.alias.items():
            self.evval[name] = self.evval[target]

    def emit_engine(self, engine, eng):
        if self.sorted_ops is None:
            self.finalize()
        for key, e, emit, wn, inc, inc_n, n_ins, event in self.sorted_ops:
            if e != engine:
                continue
            best = {}
            for w in wn:
                sem, v = self.evval[w]
                if sem.name not in best or best[sem.name][1] < v:
                    best[sem.name] = (sem, v)
            for sem, v in best.values():
                eng.wait_ge(sem, v)
            if inc is not None and n_ins > 0:
                prox = _IncEng(eng, inc, inc_n)
                emit(prox)
                assert prox.n == n_ins, f"emit count {prox.n} != {n_ins}"
            else:
                emit(eng)
                if inc is not None:
                    eng.sem_inc(inc, inc_n)


def build_nc():
    nc = bass.Bass("TRN2", target_bir_lowering=False,
                   detect_race_conditions=False)

    dx_fm = nc.dram_tensor("x_fm", [D, T], F32, kind="ExternalInput")
    dx_tm = nc.dram_tensor("x_tm", [T, D], F32, kind="ExternalInput")
    dws = {w: nc.dram_tensor(w, [D, D], F32, kind="ExternalInput")
           for w in ("wq_t", "wk_t", "wv_t", "wu_t", "wo_t", "ffn_w")}
    dfb = nc.dram_tensor("ffn_b", [128, 8], F32, kind="ExternalInput")
    dc_in = nc.dram_tensor("c_vec", [128, 8], F32, kind="ExternalInput")
    dwm = nc.dram_tensor("wmod", [D, 6 * D], F32, kind="ExternalInput")
    did_b = nc.dram_tensor("ident_bf", [128, 128], BF16, kind="ExternalInput")
    did_f = nc.dram_tensor("ident_f", [128, 128], F32, kind="ExternalInput")
    don_f = nc.dram_tensor("ones_f", [1, 128], F32, kind="ExternalInput")
    dzc_b = nc.dram_tensor("zcols_b", [128, 3], BF16, kind="ExternalInput")
    dmss = nc.dram_tensor("msk_ss", [NP, 128, 16], BF16, kind="ExternalInput")
    dmbc = nc.dram_tensor("msk_bc", [NP, 16, 128], BF16, kind="ExternalInput")
    dsel = nc.dram_tensor("sel8", [8, 8 * 128], BF16, kind="ExternalInput")
    dselr = nc.dram_tensor("sel33b", [33, 2], F32, kind="ExternalInput")
    dout = nc.dram_tensor("out_fm", [NC, 128, NT, TT], F32, kind="ExternalOutput")

    def dap(t, offset, dims):
        return bass.AP(t, offset, dims)

    BASE, CAP = 17440, 229368
    cur = [BASE]
    szof = {F32: 4, BF16: 2, I32: 4}

    def sb(name, shape, dtype, at=None):
        nbytes = int(np.prod(shape[1:])) * szof[dtype]
        off = cur[0] if at is None else at
        if at is None:
            cur[0] = off + ((nbytes + 31) // 32) * 32
        assert off + nbytes <= CAP, f"SBUF overflow: {name} @{off}+{nbytes}"
        return nc.alloc_sbuf_tensor_at(name, list(shape), dtype, offset=off)

    q_sb = sb("q_sb", [128, NC * T], BF16)
    u_sb = sb("u_sb", [128, NC * T], BF16)
    W0 = cur[0]
    wqT = sb("wqT", [128, NC * D], BF16)
    wkT = sb("wkT", [128, NC * D], BF16)
    wvT = sb("wvT", [128, NC * D], BF16)
    wuT = sb("wuT", [128, NC * D], BF16)
    woT = sb("woT", [128, NC * D], BF16, at=W0)
    fqT = sb("fqT", [128, NC * D], BF16, at=W0 + 16384)
    fwS = sb("fwS", [128, NC * D], F32, at=W0 + 32768)
    kv_sb = sb("kv_sb", [128, NP * 128], BF16)
    id_b = sb("id_b", [128, 128], BF16)
    id_f = sb("id_f", [128, 128], F32)
    on_f = sb("on_f", [1, 128], F32)
    zc_b = sb("zc_b", [128, 3], BF16)
    mss = sb("mss", [128, NP * 16], BF16)
    mbc = sb("mbc", [16, NP * 128], BF16)
    sel8 = sb("sel8", [8, 8 * 128], BF16)
    selr = sb("sel33b", [33, 2], F32)
    c_col = sb("c_col", [128, 8], F32)
    silc = sb("silc", [128, 8], F32)
    bias_c = sb("bias_c", [128, 8], F32)
    ca1 = sb("ca1", [128, 8], F32)
    sa_c = sb("sa_c", [128, 8], F32)
    ga_c = sb("ga_c", [128, 8], F32)
    cb1 = sb("cb1", [128, 8], F32)
    sb_c = sb("sb_c", [128, 8], F32)
    gb_c = sb("gb_c", [128, 8], F32)
    gsgn = sb("gsgn", [128, 16], F32)
    half_c = sb("half_c", [128, 1], F32)
    probe_s = sb("probe_s", [1, 2], F32)
    ws_c = sb("ws_c", [128, 8], F32)
    wrs_c = sb("wrs_c", [128, 8], F32)
    gws_c = sb("gws_c", [128, 8], F32)
    gbb_c = sb("gbb_c", [128, 8], F32)
    bn6 = sb("bn6", [128, 48], F32)
    bn2 = sb("bn2", [128, 2 * NG], F32)
    spack = sb("spack", [128, 8], F32)
    lrow = sb("lrow", [8, 128], BF16)
    r16 = sb("r16", [16, TT], BF16)
    rowsB = sb("rowsB", [128, TT], F32)
    ss_s = rowsB
    qrow = sb("qrow", [8, 128], BF16)
    qpack = sb("qpack", [128, 8], F32)
    amx = sb("amx", [128, NG], F32)
    negq = sb("negq", [128, 1], F32)
    ws2 = sb("ws2", [128, 8], F32)
    amx2 = sb("amx2", [128, NG], F32)
    fence = sb("fence", [128, 8], F32)

    U0 = cur[0]
    mrow = sb("mrow", [1, 6 * D], F32, at=U0)
    wmq = sb("wmq", [128, 2 * 4096], F32, at=U0 + 24576)
    x_tm = sb("x_tm", [128, NG * D], BF16, at=U0)
    x_fma = sb("x_fma", [128, NC * TT], BF16, at=U0 + 8192)
    h1r = [sb("h1a", [128, NC * TT], BF16, at=U0 + 16384),
           sb("h1b", [128, NC * TT], BF16, at=U0 + 24576)]
    h1T = sb("h1T", [128, NG * D], BF16, at=U0 + 32768)
    kTM = sb("kTM", [128, D], BF16, at=U0 + 40960)
    vTM = sb("vTM", [128, D], BF16, at=U0 + 43008)
    bcA_mu = sb("bcA_mu", [128, TT], F32, at=U0 + 45056)
    bcA_r = sb("bcA_r", [128, TT], F32, at=U0 + 47104)
    o_bf = sb("o_bf", [128, NP * TT], BF16, at=U0)
    osq = sb("osq", [128, TT], BF16, at=U0 + 8192)
    xq_i = sb("xq_i", [128, TT], I32, at=U0 + 9216)
    sgn2 = sb("sgn2", [128, TT], F32, at=U0 + 11264)
    x2 = sb("x2", [128, NC * TT], F32, at=U0 + 14336)
    h2 = sb("h2", [128, NC * TT], F32, at=U0 + 30720)
    ouxq = sb("ouxq", [128, NC * TT], BF16, at=U0 + 47104)
    xs_sb = sb("xs_sb", [128, TT], F32, at=U0 + 55296)
    x_fmb = sb("x_fmb", [128, NC * 256], F32, at=U0 + 57344)
    M_sb = sb("M_sb", [128, TT], BF16, at=U0 + 13312)
    UEND = U0 + 57344 + 8192
    wsg1 = sb("wsg1", [128, D], BF16, at=U0 + 57344)
    wqi = sb("wqi", [128, D], I32, at=U0 + 59392)
    wqb = sb("wqb", [128, NC * D], BF16, at=U0 + 14336)
    assert UEND <= CAP, f"union overflow {UEND}"

    pb = [nc.alloc_psum_tensor(f"pb{i}", [128, 512], F32) for i in range(8)]

    sch = Sched(nc)
    sD = sch.sem("sD")
    sG = sch.sem("sG")
    sT = sch.sem("sT")
    sV = sch.sem("sV")
    sS = sch.sem("sS")
    sP = sch.sem("sP")
    sO = sch.sem("sO")
    sWq = sch.sem("sWq")
    sWk = sch.sem("sWk")
    sWv = sch.sem("sWv")
    sWu = sch.sem("sWu")
    sWo = sch.sem("sWo")
    sFW = sch.sem("sFW")
    sXT = sch.sem("sXT")
    sXF = sch.sem("sXF")
    sXB = sch.sem("sXB")
    sGb = sch.sem("sGb")

    def ld(dst_ap, src_ap):
        return lambda e: e.dma_start(out=dst_ap, in_=src_ap)

    for ap_, dr, ev in ((id_b[:], did_b, "l_idb"), (id_f[:], did_f, "l_idf"),
                        (on_f[:], don_f, "l_ones"), (zc_b[:], dzc_b, "l_zcb")):
        sch.op("sync", ld(ap_, dr.ap()), inc=sD, event=ev, inc_n=16)
    sch.op("sync", ld(mss[:], dap(dmss, 0, [[16, 128], [128 * 16, NP], [1, 16]])),
           inc=sD, event="l_mss", inc_n=16)
    sch.op("sync", ld(mbc[:], dap(dmbc, 0, [[128, 16], [16 * 128, NP], [1, 128]])),
           inc=sD, event="l_mbc", inc_n=16)
    sch.op("sync", ld(sel8[:], dsel.ap()), inc=sD, event="l_sel", inc_n=16)
    sch.op("sync", ld(selr[:], dselr.ap()), inc=sD, event="l_selr", inc_n=16)
    sch.op("sync", ld(c_col[:], dc_in.ap()), inc=sD, event="l_c", inc_n=16)
    sch.op("sync", ld(bias_c[:], dfb.ap()), inc=sD, event="l_bias", inc_n=16)
    for _ev in ("l_idb", "l_idf", "l_ones", "l_zcb", "l_mss", "l_mbc",
                "l_sel", "l_selr", "l_c"):
        sch.alias_event(_ev, "l_bias")

    NW = 12

    def wm8(i):
        return ld(wmq[:, (i % 2) * 4096:(i % 2 + 1) * 4096],
                  dap(dwm, i * 512, [[6 * D, 128], [128 * 6 * D, NC], [1, 512]]))

    def wfull(dst, name):
        return ld(dst[:], dap(dws[name], 0, [[D, 128], [128 * D, NC], [1, D]]))

    sch.op("gpsimd", wm8(0), inc=sG, event="l_wm0", inc_n=16)
    sch.op("gpsimd", wm8(1), inc=sGb, event="l_wm1", inc_n=16)
    sch.op("gpsimd", wfull(wqT, "wq_t"), inc=sWq, event="l_wq", inc_n=16)

    def probe0(e):
        e.memset(negq[:], -0.25)
        return e.memset(half_c[:], PROBE)
    sch.op("vector", probe0, inc=sV, event="probe_m")
    sch.op("vector", lambda e: e.tensor_scalar(
               out=half_c[:], in0=half_c[:], scalar1=MAGIC, scalar2=MAGIC,
               op0=ALU.add, op1=ALU.subtract),
           waits=["probe_m"], inc=sV, event="probe_r")
    sch.op("vector", lambda e: e.tensor_scalar_mul(
               out=half_c[:], in0=half_c[:], scalar1=0.5 / PROBE),
           waits=["probe_r"], inc=sV, event="half_rdy")

    sch.op("scalar", lambda e: e.activation(out=silc[:], in_=c_col[:], func=AF.Silu),
           waits=["l_c"], inc=sS, event="silc")

    for i in range(NW):
        def mmod(i=i):
            def f(e):
                ins = None
                for dc in range(NC):
                    ins = e.matmul(pb[7][0:1, 0:512], silc[:, dc:dc + 1],
                                   wmq[:, (i % 2) * 4096 + dc * 512:
                                       (i % 2) * 4096 + dc * 512 + 512],
                                   start=(dc == 0), stop=(dc == NC - 1))
                return ins
            return f
        w = ["silc", f"l_wm{i}"]
        if i > 0:
            w.append(f"e_mod_{i - 1}")
        sch.op("tensor", mmod(), waits=w, inc=sT, event=f"m_mod_{i}")

        def emod(i=i):
            def f(e):
                return e.tensor_copy(out=mrow[:, i * 512:(i + 1) * 512],
                                     in_=pb[7][0:1, 0:512])
            return f
        sch.op("vector", emod(), waits=[f"m_mod_{i}"], inc=sV,
               event=f"e_mod_{i}")
        if i + 2 < NW:
            sch.op("gpsimd", wm8(i + 2), waits=[f"e_mod_{i}"],
                   inc=(sG if i % 2 == 0 else sGb),
                   event=f"l_wm{i + 2}", inc_n=16)
        if i == 2:
            sch.op("gpsimd", wfull(wkT, "wk_t"), inc=sWk, event="l_wk", inc_n=16)
        if i == 5:
            sch.op("gpsimd", wfull(wvT, "wv_t"), inc=sWv, event="l_wv", inc_n=16)
        if i == 8:
            sch.op("gpsimd", wfull(wuT, "wu_t"), inc=sWu, event="l_wu", inc_n=16)

    mod_cols = [sa_c, ca1, ga_c, sb_c, cb1, gb_c]
    for vi, col in enumerate(mod_cols):
        for dc in range(NC):
            seg = vi * D + dc * 128
            need = (seg + 127) // 512

            def tmod(seg=seg, dc=dc):
                def f(e):
                    return e.matmul(pb[6][:, (dc % 4) * 128:(dc % 4) * 128 + 128],
                                    mrow[:1, seg:seg + 128], id_f[:1, :],
                                    start=True, stop=True)
                return f
            w = [f"e_mod_{need}", "l_idf"]
            if vi == 0 and dc == 0:
                w.append("half_rdy")
            if dc % 4 == 0 and (vi or dc):
                w.append(f"ec_{vi}_{dc - 4}" if dc else f"ec_{vi - 1}_4")
            sch.op("tensor", tmod(), waits=w, inc=sT, event=f"tc_{vi}_{dc}")
            if dc % 4 == 3:
                def ecol(col=col, dc=dc):
                    def f(e):
                        return e.tensor_copy(
                            out=col[:, dc - 3:dc + 1],
                            in_=bass.AP(pb[6], 0, [[512, 128], [128, 4]]))
                    return f
                sch.op("vector", ecol(), waits=[f"tc_{vi}_{dc}"], inc=sV,
                       event=f"ec_{vi}_{dc - 3}")

    def fin1(e):
        e.tensor_scalar_add(out=ca1[:], in0=ca1[:], scalar1=1.0)
        return e.tensor_scalar_add(out=cb1[:], in0=cb1[:], scalar1=1.0)
    sch.op("vector", fin1, waits=[f"ec_{vi}_4" for vi in range(6)],
           inc=sV, event="fin1")
    def fin2s(e):
        e.activation(out=gsgn[:, 0:8], in_=ga_c[:], func=AF.Sign)
        e.activation(out=gsgn[:, 8:16], in_=gb_c[:], func=AF.Sign)
    sch.op("scalar", fin2s, waits=["fin1", "half_rdy"], inc=sS, event="fin2s")

    def fin2a(e):
        e.activation(out=ga_c[:], in_=ga_c[:], func=AF.Abs)
        return e.activation(out=gb_c[:], in_=gb_c[:], func=AF.Abs)
    sch.op("scalar", fin2a, waits=["fin2s"], inc=sS, event="fin2a")

    def fin2b(e):
        e.activation(out=ga_c[:], in_=ga_c[:], func=AF.Relu, bias=negq[:, 0:1])
        return e.activation(out=gb_c[:], in_=gb_c[:], func=AF.Relu,
                            bias=negq[:, 0:1])
    sch.op("scalar", fin2b, waits=["fin2a"], inc=sS, event="fin2")
    def fin3(e):
        e.scalar_tensor_tensor(out=ga_c[:], in0=ga_c[:], scalar=0.25,
                               in1=gsgn[:, 0:8], op0=ALU.add, op1=ALU.mult)
        return e.scalar_tensor_tensor(out=gb_c[:], in0=gb_c[:], scalar=0.25,
                                      in1=gsgn[:, 8:16], op0=ALU.add,
                                      op1=ALU.mult)
    sch.op("vector", fin3, waits=["fin2"], inc=sV, event="mod_rdy")

    kv_first = [True, True]

    def emit_prepA(i):
        pe = f"A{i}"
        w = ["mod_rdy"] if i == 0 else [f"A{i-1}_bn"]
        sch.op("gpsimd",
               ld(x_tm[:], dap(dx_tm, i * TT * D, [[D, 128], [128 * D, NG], [1, D]])),
               waits=w, inc=sXT, event=f"{pe}_xtm", inc_n=16)
        w = ["mod_rdy"] if i == 0 else [f"A{i-1}_h1"]
        sch.op("gpsimd",
               ld(x_fma[:], dap(dx_fm, i * TT, [[T, 128], [128 * T, NC], [1, TT]])),
               waits=w, inc=sXF, event=f"{pe}_xfm", inc_n=16)

        def bnst(e):
            for g in range(NG):
                e.bn_stats(out=bn6[:, g * 12:g * 12 + 6],
                           in_=x_tm[:, g * D:g * D + 512])
                e.bn_stats(out=bn6[:, g * 12 + 6:g * 12 + 12],
                           in_=x_tm[:, g * D + 512:(g + 1) * D])
        sch.op("vector", bnst, waits=[f"{pe}_xtm"], inc=sV, event=f"{pe}_bns")

        def bnag(e):
            for g in range(NG):
                e.bn_aggr(out=bn2[:, 2 * g:2 * g + 2],
                          in_=bn6[:, g * 12:g * 12 + 12])
        sch.op("vector", bnag, waits=[f"{pe}_bns"], inc=sV, event=f"{pe}_bn")

        sch.op("vector", lambda e: e.tensor_scalar_add(
                   out=bn2[:, 1:8:2], in0=bn2[:, 1:8:2], scalar1=1e-5),
               waits=[f"{pe}_bn"], inc=sV, event=f"{pe}_r0a")
        sch.op("vector", lambda e: e.reciprocal(out=bn2[:, 1:8:2],
                                                in_=bn2[:, 1:8:2]),
               waits=[f"{pe}_r0a"], inc=sV, event=f"{pe}_r0")
        sch.op("scalar", lambda e: e.activation(out=spack[:, 1:8:2],
                                                in_=bn2[:, 1:8:2], func=AF.Sqrt),
               waits=[f"{pe}_r0"], inc=sS, event=f"{pe}_r1")
        sch.op("vector", lambda e: e.tensor_mul(out=spack[:, 0:8:2],
                                                in0=bn2[:, 0:8:2],
                                                in1=spack[:, 1:8:2]),
               waits=[f"{pe}_r1"], inc=sV, event=f"{pe}_r2")
        w = [f"{pe}_r2", "l_idf"]
        if i:
            w.append(f"A{i-1}_bcmu")
        sch.op("tensor", lambda e: e.matmul(pb[5][0:8, 0:128], spack[:, 0:8],
                                            id_f[:], start=True, stop=True),
               waits=w, inc=sT, event=f"{pe}_pk")
        sch.op("vector", lambda e: e.tensor_copy(out=lrow[:],
                                                 in_=pb[5][0:8, 0:128]),
               waits=[f"{pe}_pk"], inc=sV, event=f"{pe}_lrow")

        def bc0(e):
            ins = None
            for g in range(NG):
                ins = e.matmul(pb[5][:, g * 128:(g + 1) * 128],
                               sel8[:, (2 * g) * 128:(2 * g + 1) * 128],
                               lrow[:], start=True, stop=True)
            return ins
        sch.op("tensor", bc0, waits=[f"{pe}_lrow", "l_sel"], inc=sT, event=f"{pe}_b0")

        def bc1(e):
            ins = None
            for g in range(NG):
                ins = e.matmul(pb[6][:, g * 128:(g + 1) * 128],
                               sel8[:, (2 * g + 1) * 128:(2 * g + 2) * 128],
                               lrow[:], start=True, stop=True)
            return ins
        w = [f"{pe}_lrow"]
        w.append("half_rdy" if i == 0 else f"A{i-1}_bcr")
        sch.op("tensor", bc1, waits=w, inc=sT, event=f"{pe}_b1")
        sch.op("vector", lambda e: e.tensor_copy(out=bcA_mu[:], in_=pb[5][:]),
               waits=[f"{pe}_b0"], inc=sV, event=f"{pe}_bcmu")
        sch.op("vector", lambda e: e.tensor_copy(out=bcA_r[:], in_=pb[6][:]),
               waits=[f"{pe}_b1"], inc=sV, event=f"{pe}_bcr")

        def xhat_m(e):
            ins = None
            for dc in range(NC):
                sl = slice(dc * TT, (dc + 1) * TT)
                ins = e.tensor_mul(out=x_fma[:, sl], in0=x_fma[:, sl],
                                   in1=bcA_r[:])
            return ins
        sch.op("vector", xhat_m, waits=[f"{pe}_bcmu", f"{pe}_bcr", f"{pe}_xfm"],
               inc=sV, event=f"{pe}_xhm")

        def xhat_s(e):
            ins = None
            for dc in range(NC):
                sl = slice(dc * TT, (dc + 1) * TT)
                ins = e.tensor_sub(out=x_fma[:, sl], in0=x_fma[:, sl],
                                   in1=bcA_mu[:])
            return ins
        sch.op("vector", xhat_s, waits=[f"{pe}_xhm"], inc=sV,
               event=f"{pe}_xhat")

        def h1f(e):
            ins = None
            for dc in range(NC):
                sl = slice(dc * TT, (dc + 1) * TT)
                ins = e.activation(out=h1r[i % 2][:, sl], in_=x_fma[:, sl],
                                   func=AF.Identity, scale=ca1[:, dc:dc + 1],
                                   bias=sa_c[:, dc:dc + 1])
            return ins
        w = [f"{pe}_xhat"]
        if i >= 2:
            w.append(f"A{i-2}_mmdone")
        sch.op("scalar", h1f, waits=w, inc=sS, event=f"{pe}_h1")

    def emit_mmA(i):
        pe = f"A{i}"
        hb = h1r[i % 2]
        for pi, (wt, dst, wev, fn, sc) in enumerate(
                ((wqT, q_sb, "l_wq", AF.Relu, 0.125),
                 (wuT, u_sb, "l_wu", AF.Silu, 1.0))):
            for ec in range(NC):
                bank = pb[2 + ec % 2]

                def mm(wt=wt, ec=ec, bank=bank, hb=hb):
                    def f(e):
                        ins = None
                        for dc in range(NC):
                            ins = e.matmul(
                                bank[:],
                                wt[:, dc * D + ec * 128:dc * D + ec * 128 + 128],
                                hb[:, dc * TT:(dc + 1) * TT],
                                start=(dc == 0), stop=(dc == NC - 1))
                        return ins
                    return f
                w = [f"{pe}_h1", wev]
                gi = pi * NC + ec
                if gi >= 2:
                    pj, ej = divmod(gi - 2, NC)
                    w.append(f"{pe}_pe{pj}_{ej}")
                elif i:
                    w.append(f"A{i-1}_kve{NG-1}_{2 + ec}")
                sch.op("tensor", mm(), waits=w, inc=sT, event=f"{pe}_p{pi}_{ec}")

                def ev(ec=ec, bank=bank, dst=dst, fn=fn, sc=sc, i=i):
                    def f(e):
                        return e.activation(
                            out=dst[:, ec * T + i * TT:ec * T + i * TT + TT],
                            in_=bank[:], func=fn, scale=sc)
                    return f
                sch.op("scalar", ev(), waits=[f"{pe}_p{pi}_{ec}"], inc=sS,
                       event=f"{pe}_pe{pi}_{ec}")
        for g in range(NG):
            for pi, (wt, dst, wev, fn, sc) in enumerate(
                    ((wkT, kTM, "l_wk", AF.Relu, 0.125),
                     (wvT, vTM, "l_wv", AF.Identity, 1.0))):
                for hf in range(2):
                    idx = g * 4 + pi * 2 + hf
                    bank = pb[2 + idx % 2]

                    def mm(wt=wt, g=g, hf=hf, bank=bank, hb=hb):
                        def f(e):
                            ins = None
                            for dc in range(NC):
                                ins = e.matmul(
                                    bank[:],
                                    hb[:, dc * TT + g * 128:dc * TT + g * 128 + 128],
                                    wt[:, dc * D + hf * 512:dc * D + (hf + 1) * 512],
                                    start=(dc == 0), stop=(dc == NC - 1))
                            return ins
                        return f
                    w = [f"{pe}_h1", wev]
                    if idx >= 2:
                        pg, r = divmod(idx - 2, 4)
                        w.append(f"{pe}_kve{pg}_{r}")
                    else:
                        w.append(f"{pe}_pe1_{NC - 2 + idx}")
                    if pi == 0 and hf == 0 and g:
                        w.append(f"{pe}_kv{g-1}")
                    sch.op("tensor", mm(), waits=w, inc=sT,
                           event=f"{pe}_kvm{g}_{pi * 2 + hf}")

                    def ev(hf=hf, bank=bank, dst=dst, fn=fn, sc=sc):
                        def f(e):
                            return e.activation(out=dst[:, hf * 512:(hf + 1) * 512],
                                                in_=bank[:], func=fn, scale=sc)
                        return f
                    sch.op("scalar", ev(), waits=[f"{pe}_kvm{g}_{pi * 2 + hf}"],
                           inc=sS, event=f"{pe}_kve{g}_{pi * 2 + hf}")

            def kv(g=g, i=i):
                def f(e):
                    ins = None
                    for p in range(NP):
                        bank = pb[p // 4]
                        ins = e.matmul(bank[:, (p % 4) * 128:(p % 4 + 1) * 128],
                                       kTM[:, p * 128:(p + 1) * 128],
                                       vTM[:, p * 128:(p + 1) * 128],
                                       start=(i == 0 and g == 0 and p % 4 == 0),
                                       stop=(i == NT - 1 and g == NG - 1 and p % 4 == 3))
                    return ins
                return f
            sch.op("tensor", kv(), waits=[f"{pe}_kve{g}_1", f"{pe}_kve{g}_3"],
                   inc=sT, event=f"{pe}_kv{g}")
        sch.op("tensor", lambda e: None,
               waits=[f"{pe}_kv{NG-1}", f"{pe}_p1_{NC-1}"],
               inc=sT, event=f"{pe}_mmdone")

    emit_prepA(0)
    for i in range(NT):
        if i + 1 < NT:
            emit_prepA(i + 1)
        emit_mmA(i)

    A_END = f"A{NT-1}_mmdone"
    sch.op("sync", ld(fwS[:], dap(dws["ffn_w"], 0, [[D, 128], [128 * D, NC], [1, D]])),
           waits=[A_END], inc=sFW, event="l_fw", inc_n=16)
    sch.op("gpsimd", wfull(woT, "wo_t"), waits=[A_END], inc=sWo,
           event="l_wo", inc_n=16)

    def famax(e):
        ins = None
        for ec in range(NC):
            ins = e.tensor_reduce(out=ws_c[:, ec:ec + 1],
                                  in_=fwS[:, ec * D:(ec + 1) * D],
                                  axis=AX, op=ALU.max,
                                  apply_absolute_value=True)
        return ins
    sch.op("vector", famax, waits=["l_fw"], inc=sV, event="f_amax")
    sch.op("gpsimd", lambda e: e.tensor_scalar_max(out=ws2[:], in0=ws_c[:],
                                                   scalar1=1e-8),
           waits=["f_amax"], inc=sP, event="f_clamp")

    sch.op("scalar", lambda e: e.activation(out=wrs_c[:], in_=ws2[:],
                                            func=AF.Ln, scale=1.0 / 127.0),
           waits=["f_clamp"], inc=sS, event="f_rsa")

    def f_rs(e):
        e.activation(out=wrs_c[:], in_=wrs_c[:], func=AF.Exp, scale=-1.0)
        return e.activation(out=ws_c[:], in_=ws2[:], func=AF.Identity,
                            scale=1.0 / 127.0)
    sch.op("scalar", f_rs, waits=["f_rsa"], inc=sS, event="f_rs")

    def fscale(e):
        e.tensor_mul(out=gws_c[:], in0=gb_c[:], in1=ws_c[:])
        return e.tensor_mul(out=gbb_c[:], in0=gb_c[:], in1=bias_c[:])
    sch.op("vector", fscale, waits=["f_rs", "mod_rdy", "l_bias"], inc=sV,
           event="f_scale")
    for ec in range(NC):
        def sgn1(ec=ec):
            def f(e):
                return e.activation(out=wsg1[:], in_=fwS[:, ec * D:(ec + 1) * D],
                                    func=AF.Sign)
            return f
        w = ["l_fw", A_END]
        if ec:
            w.append(f"f_wqv{ec - 1}")
        sch.op("scalar", sgn1(), waits=w, inc=sS, event=f"f_sg{ec}")

        def wq1a(ec=ec):
            def f(e):
                sl = slice(ec * D, (ec + 1) * D)
                return e.tensor_scalar(out=fwS[:, sl], in0=fwS[:, sl],
                                       scalar1=wrs_c[:, ec:ec + 1],
                                       scalar2=None, op0=ALU.mult)
            return f
        w = ["f_rs", f"f_sg{ec}"]
        if ec:
            w.append(f"f_wq{ec - 1}")
        sch.op("vector", wq1a(), waits=w, inc=sV, event=f"f_wqa{ec}")

        def wq1b(ec=ec):
            def f(e):
                return e.scalar_tensor_tensor(out=wqi[:], in0=wsg1[:],
                                              scalar=half_c[:, 0:1],
                                              in1=fwS[:, ec * D:(ec + 1) * D],
                                              op0=ALU.mult, op1=ALU.add)
            return f
        sch.op("vector", wq1b(), waits=[f"f_wqa{ec}", "half_rdy"],
               inc=sV, event=f"f_wqv{ec}")

        def wq2(ec=ec):
            def f(e):
                return e.copy(out=wqb[:, ec * D:(ec + 1) * D], in_=wqi[:])
            return f
        sch.op("scalar", wq2(), waits=[f"f_wqv{ec}"], inc=sS, event=f"f_wq{ec}")
    _tq_idxs = []
    for dcb in range(NC):
        for hf in range(2):
            def tq(dcb=dcb, hf=hf):
                def f(e):
                    ins = None
                    for eq in range(4):
                        ec = hf * 4 + eq
                        ins = e.matmul(
                            pb[7][:, eq * 128:(eq + 1) * 128],
                            wqb[:, ec * D + dcb * 128:ec * D + dcb * 128 + 128],
                            id_b[:], start=True, stop=True)
                    return ins
                return f
            w = [f"f_wq{hf * 4 + 3}", A_END]
            ti = dcb * 2 + hf
            if ti:
                w.append(f"f_te{(ti - 1) // 2}_{(ti - 1) % 2}")
            _tq_idxs.append(len(sch.ops))
            sch.op("tensor", tq(), waits=w, inc=sT, event=f"f_t{dcb}_{hf}")

            def tqe(dcb=dcb, hf=hf):
                def f(e):
                    return e.copy(
                        out=fqT[:, dcb * D + hf * 512:dcb * D + (hf + 1) * 512],
                        in_=pb[7][:])
                return f
            _tq_idxs.append(len(sch.ops))
            sch.op("scalar", tqe(), waits=[f"f_t{dcb}_{hf}"], inc=sS,
                   event=f"f_te{dcb}_{hf}")
    F_END = f"f_te{NC-1}_1"

    _kvz_idx = len(sch.ops)
    sch.op("gpsimd", lambda e: e.memset(kv_sb[:], 0.0), inc=sP, event="kvz")
    sch.op("gpsimd", lambda e: e.memset(rowsB[:], 0.0), waits=[A_END],
           inc=sP, event="rbz")

    def kvev(e):
        ins = None
        for p in range(NP):
            bank = pb[p // 4]
            c0 = (p % 4) * 128
            ins = e.tensor_copy(out=kv_sb[0:64, p * 128:p * 128 + 64],
                                in_=bank[0:64, c0:c0 + 64])
            ins = e.tensor_copy(out=kv_sb[64:128, p * 128 + 64:(p + 1) * 128],
                                in_=bank[64:128, c0 + 64:c0 + 128])
        return ins
    _kvev_idx = len(sch.ops)
    sch.op("vector", kvev, waits=[A_END, "kvz"], inc=sV, event="kv_rdy")
    _b0f = []

    _P_B0 = float(len(sch.ops))
    for i in range(NT):
        pe = f"B{i}"
        for p in range(NP):
            def om(p=p, i=i):
                def f(e):
                    return e.matmul(pb[p % 2][:], kv_sb[:, p * 128:(p + 1) * 128],
                                    q_sb[:, p * T + i * TT:p * T + i * TT + TT],
                                    start=True, stop=True)
                return f
            w = ["kv_rdy"]
            if p >= 2:
                w.append(f"{pe}_oe{p - 2}")
            elif i:
                w += [f"B{i-1}_oe{NP - 2 + p}", f"B{i-1}_ou{NP - 1}"]
            if i == 0:
                _b0f.append(len(sch.ops))
            sch.op("tensor", om(), waits=w, inc=sT, event=f"{pe}_o{p}")

            def oev(p=p):
                def f(e):
                    e.activation(out=o_bf[:, p * TT:(p + 1) * TT],
                                 in_=pb[p % 2][:], func=AF.Identity)
                    return e.activation(out=osq[:], in_=pb[p % 2][:], func=AF.Square)
                return f
            w = [f"{pe}_o{p}"]
            if p:
                w.append(f"{pe}_ss{p - 1}")
            if p == 0 and i == 0:
                w.append(A_END)
            if i == 0:
                _b0f.append(len(sch.ops))
            sch.op("scalar", oev(), waits=w, inc=sS, event=f"{pe}_oe{p}")

            def ssm(p=p):
                def f(e):
                    return e.matmul(pb[2][0:16, :], mss[:, p * 16:(p + 1) * 16],
                                    osq[:], start=(p == 0), stop=(p == NP - 1))
                return f
            w = [f"{pe}_oe{p}", "l_mss"]
            if p == 0 and i:
                w.append(f"B{i-1}_sse")
            if p == 0 and i == 0:
                w.append(f"A{NT-1}_kve{NG-1}_2")
            if i == 0:
                _b0f.append(len(sch.ops))
            sch.op("tensor", ssm(), waits=w, inc=sT, event=f"{pe}_ss{p}")
        w = [f"{pe}_ss{NP-1}", "rbz"]
        if i:
            w.append(f"B{i-1}_lb1")
            w.append(f"B{i-1}_lb0")
        if i == 0:
            _b0f.append(len(sch.ops))
        sch.op("vector", lambda e, w=w: e.tensor_copy(out=ss_s[0:16, :], in_=pb[2][0:16, :]),
               waits=w, inc=sV, event=f"{pe}_sse")
        if i == 0:
            _b0f.append(len(sch.ops))
        sch.op("scalar", lambda e: e.activation(out=ss_s[0:16, :], in_=ss_s[0:16, :], func=AF.Sqrt),
               waits=[f"{pe}_sse"], inc=sS, event=f"{pe}_sq")

        if i == 0:
            _b0f.append(len(sch.ops))
        sch.op("vector", lambda e: e.tensor_scalar_max(
                   out=ss_s[0:16, :], in0=ss_s[0:16, :], scalar1=8e-12),
               waits=[f"{pe}_sq"], inc=sV, event=f"{pe}_sr0")
        if i == 0:
            _b0f.append(len(sch.ops))
        sch.op("vector", lambda e: e.reciprocal(out=ss_s[0:16, :],
                                                in_=ss_s[0:16, :]),
               waits=[f"{pe}_sr0"], inc=sV, event=f"{pe}_sr1")
        if i == 0:
            _b0f.append(len(sch.ops))
        sch.op("vector", lambda e: e.tensor_scalar(
                   out=r16[:], in0=ss_s[0:16, :], scalar1=8.0,
                   scalar2=None, op0=ALU.mult),
               waits=[f"{pe}_sr1"], inc=sV, event=f"{pe}_r16")
        for p in range(NP):
            bank = pb[3 if p % 2 == 0 else 6]

            def rbm(p=p, bank=bank):
                def f(e):
                    return e.matmul(bank[:], mbc[:, p * 128:(p + 1) * 128],
                                    r16[:], start=True, stop=True)
                return f
            w = [f"{pe}_r16", "l_mbc"]
            if p >= 2:
                w.append(f"{pe}_ou{p - 2}")
            elif i:
                w.append(f"B{i-1}_xqv{NC - 1}" if p == 0 else f"B{i-1}_xse")
            if i == 0:
                _b0f.append(len(sch.ops))
            sch.op("tensor", rbm(), waits=w, inc=sT, event=f"{pe}_rb{p}")

            def ouf1(p=p, bank=bank):
                def f(e):
                    return e.tensor_mul(out=ouxq[:, p * TT:(p + 1) * TT],
                                        in0=o_bf[:, p * TT:(p + 1) * TT],
                                        in1=bank[:])
                return f
            w = [f"{pe}_rb{p}", f"{pe}_oe{p}"]
            if p == 0 and i > 0:
                w.append(f"B{i-1}_xqc{NC - 1}")
            if i == 0:
                _b0f.append(len(sch.ops))
            sch.op("vector", ouf1(), waits=w, inc=sV, event=f"{pe}_oa{p}")

            def ouf2(p=p, i=i):
                def f(e):
                    return e.tensor_mul(
                        out=ouxq[:, p * TT:(p + 1) * TT],
                        in0=ouxq[:, p * TT:(p + 1) * TT],
                        in1=u_sb[:, p * T + i * TT:p * T + i * TT + TT])
                return f
            if i == 0:
                _b0f.append(len(sch.ops))
            sch.op("vector", ouf2(), waits=[f"{pe}_oa{p}"], inc=sV,
                   event=f"{pe}_ou{p}")
        for ec in range(NC):
            bank = pb[4 + ec % 2]

            def wom(ec=ec, bank=bank):
                def f(e):
                    ins = None
                    for dc in range(NC):
                        ins = e.matmul(
                            bank[:],
                            woT[:, dc * D + ec * 128:dc * D + ec * 128 + 128],
                            ouxq[:, dc * TT:(dc + 1) * TT],
                            start=(dc == 0), stop=(dc == NC - 1))
                    return ins
                return f
            w = [f"{pe}_ou{NP - 1}", "l_wo"]
            if ec >= 2:
                w.append(f"{pe}_x2c{ec - 2}")
            elif i:
                w.append(f"B{i-1}_ff{NC - 2 + ec}")
            elif ec != 0:
                w.append(f"A{NT-1}_bcmu")
            if i == 0:
                _b0f.append(len(sch.ops))
            sch.op("tensor", wom(), waits=w, inc=sT, event=f"{pe}_wo{ec}")

            def x2f(ec=ec, bank=bank):
                def f(e):
                    return e.tensor_scalar(out=x2[:, ec * TT:(ec + 1) * TT],
                                           in0=bank[:], scalar1=ga_c[:, ec:ec + 1],
                                           scalar2=None, op0=ALU.mult)
                return f
            w = [f"{pe}_wo{ec}"]
            if ec == 0:
                w.append(F_END if i == 0 else f"B{i-1}_x3{NC-1}")
            sch.op("vector", x2f(), waits=w, inc=sV, event=f"{pe}_x2c{ec}")
        for hf in range(2):
            w = []
            if i == 0 and hf == 0:
                w.append(f"f_wq{NC - 1}")
            if i or hf:
                w.append(f"{pe}_xa{hf-1}" if hf else f"B{i-1}_xa1")
            if hf == 0 and i:
                w.append(f"B{i-1}_xa1")
            sch.op("sync", ld(x_fmb[:], dap(dx_fm, i * TT + hf * 256,
                                            [[T, 128], [128 * T, NC], [1, 256]])),
                   waits=w, inc=sXB, event=f"{pe}_xf{hf}", inc_n=16)

            def xadd(hf=hf):
                def f(e):
                    ins = None
                    for ec in range(NC):
                        sl = slice(ec * TT + hf * 256, ec * TT + hf * 256 + 256)
                        ins = e.tensor_add(out=x2[:, sl], in0=x2[:, sl],
                                           in1=x_fmb[:, ec * 256:(ec + 1) * 256])
                    return ins
                return f
            sch.op("vector", xadd(), waits=[f"{pe}_xf{hf}", f"{pe}_x2c{NC-1}"],
                   inc=sV, event=f"{pe}_xa{hf}")
        def x2bf(e):
            return e.activation(out=ouxq[:], in_=x2[:], func=AF.Identity)
        sch.op("scalar", x2bf, waits=[f"{pe}_xa1", f"{pe}_wo{NC-1}"],
               inc=sS, event=f"{pe}_x2b")

        def x2sq(e):
            return e.tensor_mul(out=o_bf[:], in0=x2[:], in1=x2[:])
        sch.op("vector", x2sq, waits=[f"{pe}_xa1", f"{pe}_ou{NP-1}"],
               inc=sV, event=f"{pe}_x2q")

        def lnm(e):
            ins = None
            for dc in range(NC):
                ins = e.matmul(pb[7][0:1, :], zc_b[:, 0:1],
                               ouxq[:, dc * TT:(dc + 1) * TT],
                               start=(dc == 0), stop=(dc == NC - 1))
            for dc in range(NC):
                ins = e.matmul(pb[6][0:1, :], zc_b[:, 0:1],
                               o_bf[:, dc * TT:(dc + 1) * TT],
                               start=(dc == 0), stop=(dc == NC - 1))
            return ins
        w = [f"{pe}_x2b", f"{pe}_x2q", "l_zcb", f"{pe}_ss{NP-1}",
             f"{pe}_ou{NP-1}"]
        if i:
            w.append(f"B{i-1}_qr")
        else:
            w.append(F_END)
        sch.op("tensor", lnm, waits=w, inc=sT, event=f"{pe}_l2m")

        def l2ev(e):
            e.tensor_copy(out=rowsB[0:1, :], in_=pb[7][0:1, :])
            return e.tensor_copy(out=rowsB[32:33, :], in_=pb[6][0:1, :])
        sch.op("vector", l2ev, waits=[f"{pe}_l2m"], inc=sV, event=f"{pe}_l2ev")

        def l2g(e):
            ins = None
            for g in range(NG):
                ins = e.matmul(pb[7][:, 8 + g * 2:8 + g * 2 + 2],
                               rowsB[0:33, g * 128:(g + 1) * 128], selr[:],
                               start=True, stop=True)
            return ins
        sch.op("tensor", l2g, waits=[f"{pe}_l2ev", "l_selr"], inc=sT,
               event=f"{pe}_l2g")

        sch.op("vector", lambda e: e.tensor_scalar_mul(
                   out=bn2[:], in0=pb[7][:, 8:16], scalar1=1.0 / D),
               waits=[f"{pe}_l2g"], inc=sV, event=f"{pe}_l2ca")
        sch.op("vector", lambda e: e.tensor_mul(
                   out=spack[:, 0:8:2], in0=bn2[:, 0:8:2], in1=bn2[:, 0:8:2]),
               waits=[f"{pe}_l2ca"], inc=sV, event=f"{pe}_l2cb")
        sch.op("vector", lambda e: e.tensor_sub(
                   out=bn2[:, 1:8:2], in0=bn2[:, 1:8:2], in1=spack[:, 0:8:2]),
               waits=[f"{pe}_l2cb"], inc=sV, event=f"{pe}_l2cc")
        sch.op("vector", lambda e: e.tensor_scalar_add(
                   out=bn2[:, 1:8:2], in0=bn2[:, 1:8:2], scalar1=1e-5),
               waits=[f"{pe}_l2cc"], inc=sV, event=f"{pe}_l2cd")
        sch.op("vector", lambda e: e.reciprocal(out=bn2[:, 1:8:2],
                                                in_=bn2[:, 1:8:2]),
               waits=[f"{pe}_l2cd"], inc=sV, event=f"{pe}_l2c")
        sch.op("scalar", lambda e: e.activation(out=spack[:, 1:8:2],
                                                in_=bn2[:, 1:8:2], func=AF.Sqrt),
               waits=[f"{pe}_l2c"], inc=sS, event=f"{pe}_l2s")
        sch.op("vector", lambda e: e.tensor_mul(out=spack[:, 0:8:2],
                                                in0=bn2[:, 0:8:2],
                                                in1=spack[:, 1:8:2]),
               waits=[f"{pe}_l2s"], inc=sV, event=f"{pe}_l2c2")
        sch.op("tensor", lambda e: e.matmul(pb[7][0:8, 0:128], spack[:, 0:8],
                                            id_f[:], start=True, stop=True),
               waits=[f"{pe}_l2c2"], inc=sT, event=f"{pe}_l2pk")
        sch.op("vector", lambda e: e.tensor_copy(out=lrow[:],
                                                 in_=pb[7][0:8, 0:128]),
               waits=[f"{pe}_l2pk"], inc=sV, event=f"{pe}_l2e")
        def l2b0(e):
            ins = None
            for g in range(NG):
                ins = e.matmul(pb[3][:, g * 128:(g + 1) * 128],
                               sel8[:, (2 * g + 1) * 128:(2 * g + 2) * 128],
                               lrow[:], start=True, stop=True)
            return ins
        sch.op("tensor", l2b0, waits=[f"{pe}_l2e", f"{pe}_ou{NP-1}", "l_selr"],
               inc=sT, event=f"{pe}_lb0")

        def l2b1(e):
            ins = None
            for g in range(NG):
                ins = e.matmul(pb[6][:, g * 128:(g + 1) * 128],
                               sel8[:, (2 * g) * 128:(2 * g + 1) * 128],
                               lrow[:], start=True, stop=True)
            return ins
        sch.op("tensor", l2b1, waits=[f"{pe}_l2e", f"{pe}_ou{NP-1}"], inc=sT,
               event=f"{pe}_lb1")
        def xh2m(e):
            for dc in range(NC):
                sl = slice(dc * TT, (dc + 1) * TT)
                e.tensor_mul(out=h2[:, sl], in0=x2[:, sl], in1=pb[3][:])
        w = [f"{pe}_lb0", f"{pe}_lb1", f"{pe}_xa1"]
        w.append(F_END if i == 0 else f"B{i-1}_outd")
        sch.op("vector", xh2m, waits=w, inc=sV, event=f"{pe}_xh2m")

        def xh2s(e):
            for dc in range(NC):
                sl = slice(dc * TT, (dc + 1) * TT)
                e.tensor_sub(out=h2[:, sl], in0=h2[:, sl], in1=pb[6][:])
        sch.op("vector", xh2s, waits=[f"{pe}_xh2m"], inc=sV, event=f"{pe}_xh2")

        def h2f(e):
            ins = None
            for dc in range(NC):
                sl = slice(dc * TT, (dc + 1) * TT)
                ins = e.activation(out=h2[:, sl], in_=h2[:, sl], func=AF.Identity,
                                   scale=cb1[:, dc:dc + 1], bias=sb_c[:, dc:dc + 1])
            return ins
        sch.op("scalar", h2f, waits=[f"{pe}_xh2"], inc=sS, event=f"{pe}_h2")
        def hamax(e):
            return e.tensor_reduce(
                out=M_sb[:, 0:TT],
                in_=h2[:].rearrange("p (c t) -> p t c", c=NC),
                axis=AX, op=ALU.max, apply_absolute_value=True)
        sch.op("vector", hamax, waits=[f"{pe}_h2", f"{pe}_xa1"], inc=sV,
               event=f"{pe}_amx")
        def mtr(e):
            ins = None
            for g in range(NG):
                ins = e.matmul(pb[7][:, g * 128:(g + 1) * 128],
                               M_sb[:, g * 128:(g + 1) * 128], id_b[:],
                               start=True, stop=True)
            return ins
        sch.op("tensor", mtr, waits=[f"{pe}_amx", f"{pe}_l2e"], inc=sT,
               event=f"{pe}_mtr")

        def ared(e):
            ins = None
            for g in range(NG):
                ins = e.tensor_reduce(out=amx[:, g:g + 1],
                                      in_=pb[7][:, g * 128:(g + 1) * 128],
                                      axis=AX, op=ALU.max)
            return ins
        w = [f"{pe}_mtr"]
        if i:
            w.append(f"B{i-1}_am2")
        sch.op("vector", ared, waits=w, inc=sV, event=f"{pe}_am")
        w = [f"{pe}_am"]
        if i:
            w.append(f"B{i-1}_qs")
        sch.op("gpsimd", lambda e: e.tensor_scalar_max(out=amx2[:], in0=amx[:],
                                                       scalar1=1e-8),
               waits=w, inc=sP, event=f"{pe}_am2")

        w = [f"{pe}_am2"]
        if i:
            w.append(f"B{i-1}_qt")
        sch.op("scalar", lambda e: e.activation(out=qpack[:, 0:4],
                                                in_=amx2[:], func=AF.Ln,
                                                scale=1.0 / 127.0),
               waits=w, inc=sS, event=f"{pe}_qsa")

        def qsc(e):
            e.activation(out=qpack[:, 0:4], in_=qpack[:, 0:4], func=AF.Exp,
                         scale=-1.0)
            return e.activation(out=qpack[:, 4:8], in_=amx2[:],
                                func=AF.Identity, scale=1.0 / 127.0)
        sch.op("scalar", qsc, waits=[f"{pe}_qsa"], inc=sS, event=f"{pe}_qs")
        sch.op("tensor", lambda e: e.matmul(pb[7][0:8, 0:128], qpack[:, 0:8],
                                            id_f[:], start=True, stop=True),
               waits=[f"{pe}_qs"], inc=sT, event=f"{pe}_qt")
        sch.op("scalar", lambda e: e.copy(out=qrow[:], in_=pb[7][0:8, 0:128]),
               waits=[f"{pe}_qt"], inc=sS, event=f"{pe}_qr")
        def qb0(e):
            ins = None
            for g in range(NG):
                ins = e.matmul(pb[3][:, g * 128:(g + 1) * 128],
                               sel8[:, g * 128:(g + 1) * 128],
                               qrow[:], start=True, stop=True)
            return ins
        sch.op("tensor", qb0, waits=[f"{pe}_qr", f"{pe}_xh2"], inc=sT,
               event=f"{pe}_qb0")

        def qb1(e):
            ins = None
            for g in range(NG):
                ins = e.matmul(pb[6][:, g * 128:(g + 1) * 128],
                               sel8[:, (4 + g) * 128:(5 + g) * 128],
                               qrow[:], start=True, stop=True)
            return ins
        sch.op("tensor", qb1, waits=[f"{pe}_qr", f"{pe}_xh2"], inc=sT,
               event=f"{pe}_qb1")
        w = [f"{pe}_qb1"]
        if i == 0:
            w.append(F_END)
        else:
            w.append(f"B{i-1}_ffe{NC-1}")
        sch.op("scalar", lambda e: e.copy(out=xs_sb[:], in_=pb[6][:]),
               waits=w, inc=sS, event=f"{pe}_xse")
        sch.op("scalar", lambda e: e.activation(out=o_bf[:], in_=h2[:],
                                                func=AF.Sign),
               waits=[f"{pe}_h2", f"{pe}_l2m"], inc=sS, event=f"{pe}_xqs")
        for dc in range(NC):
            def xqv(dc=dc):
                def f(e):
                    return e.tensor_mul(out=h2[:, dc * TT:(dc + 1) * TT],
                                        in0=h2[:, dc * TT:(dc + 1) * TT], in1=pb[3][:])
                return f
            w = [f"{pe}_qb0", f"{pe}_h2", f"{pe}_xqs"]
            sch.op("vector", xqv(), waits=w, inc=sV, event=f"{pe}_xqv{dc}")

            def xqr(dc=dc):
                def f(e):
                    return e.scalar_tensor_tensor(
                        out=xq_i[:], in0=o_bf[:, dc * TT:(dc + 1) * TT],
                        scalar=half_c[:, 0:1],
                        in1=h2[:, dc * TT:(dc + 1) * TT],
                        op0=ALU.mult, op1=ALU.add)
                return f
            w = [f"{pe}_xqv{dc}", f"{pe}_xqs"]
            if dc:
                w.append(f"{pe}_xqc{dc - 1}")
            sch.op("vector", xqr(), waits=w, inc=sV, event=f"{pe}_xqi{dc}")

            def xqc2(dc=dc):
                def f(e):
                    return e.copy(out=ouxq[:, dc * TT:(dc + 1) * TT],
                                  in_=xq_i[:])
                return f
            w = [f"{pe}_xqi{dc}", f"{pe}_l2m"]
            sch.op("scalar", xqc2(), waits=w, inc=sS, event=f"{pe}_xqc{dc}")
        def x2p(e):
            ins = None
            for ec in range(NC):
                ins = e.activation(out=x2[:, ec * TT:(ec + 1) * TT],
                                   in_=x2[:, ec * TT:(ec + 1) * TT],
                                   func=AF.Identity, bias=gbb_c[:, ec:ec + 1])
            return ins
        sch.op("scalar", x2p, waits=[f"{pe}_xh2", f"{pe}_x2q", "f_scale"], inc=sS,
               event=f"{pe}_x2p")
        for ec in range(NC):
            bank = pb[4 + ec % 2]

            def ffm(ec=ec, bank=bank):
                def f(e):
                    ins = None
                    for dc in range(NC):
                        ins = e.matmul(
                            bank[:],
                            fqT[:, dc * D + ec * 128:dc * D + ec * 128 + 128],
                            ouxq[:, dc * TT:(dc + 1) * TT],
                            start=(dc == 0), stop=(dc == NC - 1))
                    return ins
                return f
            w = [f"{pe}_xqc{NC-1}"]
            if ec >= 2:
                w.append(f"{pe}_ffe{ec - 2}")
            else:
                w.append(f"{pe}_x2c{NC - 1}")
            if i == 0:
                w.append(F_END)
            sch.op("tensor", ffm(), waits=w, inc=sT, event=f"{pe}_ff{ec}")

            def ffev(ec=ec, bank=bank):
                def f(e):
                    return e.scalar_tensor_tensor(
                        out=h2[:, ec * TT:(ec + 1) * TT], in0=bank[:],
                        scalar=gws_c[:, ec:ec + 1], in1=xs_sb[:],
                        op0=ALU.mult, op1=ALU.mult)
                return f
            sch.op("vector", ffev(), waits=[f"{pe}_ff{ec}", f"{pe}_xse", "f_scale",
                                            f"{pe}_xqv{ec}"],
                   inc=sV, event=f"{pe}_ffe{ec}")

            def x3f(ec=ec):
                def f(e):
                    return e.tensor_add(out=h2[:, ec * TT:(ec + 1) * TT],
                                        in0=h2[:, ec * TT:(ec + 1) * TT],
                                        in1=x2[:, ec * TT:(ec + 1) * TT])
                return f
            sch.op("gpsimd", x3f(), waits=[f"{pe}_ffe{ec}", f"{pe}_x2p"],
                   inc=sP, event=f"{pe}_x3{ec}")
        def store(i=i):
            def f(e):
                return e.dma_start(
                    out=dap(dout, i * TT, [[NT * TT, 128], [128 * NT * TT, NC], [1, TT]]),
                    in_=h2[:])
            return f
        sch.op("sync", store(), waits=[f"{pe}_x3{ec2}" for ec2 in range(NC)],
               inc=sO, event=f"{pe}_out", inc_n=16)
        sch.op("sync", lambda e: None, waits=[f"{pe}_out"],
               inc=sD, event=f"{pe}_outd")

    for _j, _oi in enumerate(_tq_idxs):
        sch.ops[_oi][0] = _P_B0 + 23.5 + _j * 0.001
    _K0 = _tq_idxs[0] - 40.0
    sch.ops[_kvz_idx][0] = _K0 - 0.3
    sch.ops[_kvz_idx + 1][0] = _K0 - 0.2
    sch.ops[_kvev_idx][0] = _K0 - 0.1
    for _j, _oi in enumerate(_b0f[:-8]):
        sch.ops[_oi][0] = _K0 + _j * 0.01


    sch.op("sync", lambda e: None, waits=[f"B{NT-1}_outd"], inc=sD,
           event="all_done")

    with nc.Block() as block:
        @block.sync
        def _(eng):
            sch.emit_engine("sync", eng)

        @block.gpsimd
        def _(eng):
            sch.emit_engine("gpsimd", eng)

        @block.vector
        def _(eng):
            sch.emit_engine("vector", eng)

        @block.scalar
        def _(eng):
            sch.emit_engine("scalar", eng)

        @block.tensor
        def _(eng):
            sch.emit_engine("tensor", eng)

    return nc


def _host_aux():
    ident = np.eye(128)
    msk_ss = np.zeros((NP, 128, 16), dtype=np.float32)
    msk_bc = np.zeros((NP, 16, 128), dtype=np.float32)
    for p in range(NP):
        for r in range(128):
            h = 2 * p + (1 if r >= 64 else 0)
            msk_ss[p, r, h] = 1.0
            msk_bc[p, h, r] = 1.0
    zc = np.zeros((128, 3), dtype=np.float32)
    zc[:, 0] = 1.0
    zc[:, 2] = 1.0
    import ml_dtypes
    bf = ml_dtypes.bfloat16
    return {
        "ident_bf": ident.astype(bf),
        "ident_f": ident.astype(np.float32),
        "ones_f": np.ones((1, 128), dtype=np.float32),
        "zcols_b": zc.astype(bf),
        "msk_ss": msk_ss.astype(bf),
        "msk_bc": msk_bc.astype(bf),
        "sel8": np.concatenate(
            [np.outer((np.arange(8) == r).astype(np.float32),
                      np.ones(128, np.float32)) for r in range(8)],
            axis=1).astype(bf),
        "sel33b": np.stack([(np.arange(33) == 0).astype(np.float32),
                            (np.arange(33) == 32).astype(np.float32)], axis=1),
    }


def kernel(L0_attn_x, L0_attn_Wq, L0_attn_Wk, L0_attn_Wv, L0_attn_Wu,
           L0_attn_Wo, L0_ffn_weight, L0_ffn_bias, L0_c, L0_wmod):
    x = np.ascontiguousarray(np.asarray(L0_attn_x, dtype=np.float32))
    aux = _host_aux()
    shared = {
        "wq_t": np.ascontiguousarray(np.asarray(L0_attn_Wq, np.float32).T),
        "wk_t": np.ascontiguousarray(np.asarray(L0_attn_Wk, np.float32).T),
        "wv_t": np.ascontiguousarray(np.asarray(L0_attn_Wv, np.float32).T),
        "wu_t": np.ascontiguousarray(np.asarray(L0_attn_Wu, np.float32).T),
        "wo_t": np.ascontiguousarray(np.asarray(L0_attn_Wo, np.float32).T),
        "ffn_w": np.ascontiguousarray(np.asarray(L0_ffn_weight, np.float32)),
        "ffn_b": np.ascontiguousarray(
            np.asarray(L0_ffn_bias, np.float32).reshape(8, 128).T),
        "wmod": np.ascontiguousarray(np.asarray(L0_wmod, np.float32)),
        **aux,
    }
    carr = np.asarray(L0_c, np.float32)
    in_maps = []
    for b in range(8):
        m = dict(shared)
        m["x_fm"] = np.ascontiguousarray(x[b].T)
        m["x_tm"] = np.ascontiguousarray(x[b])
        m["c_vec"] = np.ascontiguousarray(carr[b].reshape(8, 128).T)
        in_maps.append(m)
    nc = build_nc()
    res = run_bass_kernel_spmd(nc, in_maps, core_ids=list(range(8)))
    out = np.empty((8, T, D), dtype=np.float32)
    for b in range(8):
        o = np.asarray(res.results[b]["out_fm"])
        out[b] = o.reshape(D, NT * TT).T
    return out


# version 17
# speedup vs baseline: 1.0389x; 1.0186x over previous
import numpy as np
import concourse.bass as bass
import concourse.mybir as mybir
from concourse.bass_utils import run_bass_kernel_spmd

F32 = mybir.dt.float32
BF16 = mybir.dt.bfloat16
I32 = mybir.dt.int32
AF = mybir.ActivationFunctionType
ALU = mybir.AluOpType
AX = mybir.AxisListType.X

D = 1024
T = 2048
NT = 4
TT = 512
NG = 4
NC = 8
NP = 8
MAGIC = 12582912.0
PROBE = 0.3


class _MockIns:
    def then_inc(self, *a, **kw):
        return self


class _CountEng:

    def __init__(self):
        self.n = 0

    def wait_ge(self, *a, **kw):
        pass

    def sem_inc(self, *a, **kw):
        pass

    def __getattr__(self, name):
        def f(*a, **kw):
            self.n += 1
            return _MockIns()
        return f


class _IncEng:

    def __init__(self, eng, sem, inc_n):
        self._eng = eng
        self._sem = sem
        self._inc_n = inc_n
        self.n = 0

    def wait_ge(self, *a, **kw):
        return self._eng.wait_ge(*a, **kw)

    def sem_inc(self, *a, **kw):
        return self._eng.sem_inc(*a, **kw)

    def __getattr__(self, name):
        f = getattr(self._eng, name)
        if not callable(f):
            return f

        def wrap(*a, **kw):
            ins = f(*a, **kw)
            if ins is not None and hasattr(ins, "then_inc"):
                ins.then_inc(self._sem, self._inc_n)
                self.n += 1
            return ins
        return wrap


class Sched:

    def __init__(self, nc):
        self.nc = nc
        self.ops = []
        self.evset = set()
        self.alias = {}
        self.sorted_ops = None

    def sem(self, name):
        return self.nc.alloc_semaphore(name)

    def op(self, engine, emit, waits=(), inc=None, event=None, inc_n=1,
           key=None):
        for w in waits:
            assert w in self.evset, f"unknown event {w}"
        mock = _CountEng()
        emit(mock)
        n_ins = mock.n
        if event is not None:
            assert inc is not None
            assert event not in self.evset, f"dup event {event}"
            self.evset.add(event)
        if key is None:
            key = float(len(self.ops))
        self.ops.append([key, engine, emit, tuple(waits), inc, inc_n,
                         n_ins, event])

    def alias_event(self, name, target):
        assert target in self.evset and name in self.evset
        self.alias[name] = target

    def finalize(self):
        self.sorted_ops = sorted(self.ops, key=lambda t: t[0])
        counts = {}
        self.evval = {}
        for key, eng, emit, wn, inc, inc_n, n_ins, event in self.sorted_ops:
            if inc is not None:
                c = counts.get(inc.name, 0) + inc_n * max(n_ins, 1)
                counts[inc.name] = c
                if event is not None:
                    self.evval[event] = (inc, c)
        for name, target in self.alias.items():
            self.evval[name] = self.evval[target]

    def emit_engine(self, engine, eng):
        if self.sorted_ops is None:
            self.finalize()
        for key, e, emit, wn, inc, inc_n, n_ins, event in self.sorted_ops:
            if e != engine:
                continue
            best = {}
            for w in wn:
                sem, v = self.evval[w]
                if sem.name not in best or best[sem.name][1] < v:
                    best[sem.name] = (sem, v)
            for sem, v in best.values():
                eng.wait_ge(sem, v)
            if inc is not None and n_ins > 0:
                prox = _IncEng(eng, inc, inc_n)
                emit(prox)
                assert prox.n == n_ins, f"emit count {prox.n} != {n_ins}"
            else:
                emit(eng)
                if inc is not None:
                    eng.sem_inc(inc, inc_n)


def build_nc():
    nc = bass.Bass("TRN2", target_bir_lowering=False,
                   detect_race_conditions=False)

    dx_fm = nc.dram_tensor("x_fm", [D, T], F32, kind="ExternalInput")
    dx_tm = nc.dram_tensor("x_tm", [T, D], F32, kind="ExternalInput")
    dws = {w: nc.dram_tensor(w, [D, D], F32, kind="ExternalInput")
           for w in ("wq_t", "wk_t", "wv_t", "wu_t", "wo_t", "ffn_w")}
    dfb = nc.dram_tensor("ffn_b", [128, 8], F32, kind="ExternalInput")
    dc_in = nc.dram_tensor("c_vec", [128, 8], F32, kind="ExternalInput")
    dwm = nc.dram_tensor("wmod", [D, 6 * D], F32, kind="ExternalInput")
    did_b = nc.dram_tensor("ident_bf", [128, 128], BF16, kind="ExternalInput")
    did_f = nc.dram_tensor("ident_f", [128, 128], F32, kind="ExternalInput")
    don_f = nc.dram_tensor("ones_f", [1, 128], F32, kind="ExternalInput")
    dzc_b = nc.dram_tensor("zcols_b", [128, 3], BF16, kind="ExternalInput")
    dmss = nc.dram_tensor("msk_ss", [NP, 128, 16], BF16, kind="ExternalInput")
    dmbc = nc.dram_tensor("msk_bc", [NP, 16, 128], BF16, kind="ExternalInput")
    dsel = nc.dram_tensor("sel8", [8, 8 * 128], BF16, kind="ExternalInput")
    dselr = nc.dram_tensor("sel33b", [33, 2], F32, kind="ExternalInput")
    dout = nc.dram_tensor("out_fm", [NC, 128, NT, TT], F32, kind="ExternalOutput")

    def dap(t, offset, dims):
        return bass.AP(t, offset, dims)

    BASE, CAP = 17440, 229368
    cur = [BASE]
    szof = {F32: 4, BF16: 2, I32: 4}

    def sb(name, shape, dtype, at=None):
        nbytes = int(np.prod(shape[1:])) * szof[dtype]
        off = cur[0] if at is None else at
        if at is None:
            cur[0] = off + ((nbytes + 31) // 32) * 32
        assert off + nbytes <= CAP, f"SBUF overflow: {name} @{off}+{nbytes}"
        return nc.alloc_sbuf_tensor_at(name, list(shape), dtype, offset=off)

    q_sb = sb("q_sb", [128, NC * T], BF16)
    u_sb = sb("u_sb", [128, NC * T], BF16)
    W0 = cur[0]
    wqT = sb("wqT", [128, NC * D], BF16)
    wkT = sb("wkT", [128, NC * D], BF16)
    wvT = sb("wvT", [128, NC * D], BF16)
    wuT = sb("wuT", [128, NC * D], BF16)
    woT = sb("woT", [128, NC * D], BF16, at=W0)
    fqT = sb("fqT", [128, NC * D], BF16, at=W0 + 16384)
    fwS = sb("fwS", [128, NC * D], F32, at=W0 + 32768)
    kv_sb = sb("kv_sb", [128, NP * 128], BF16)
    id_b = sb("id_b", [128, 128], BF16)
    id_f = sb("id_f", [128, 128], F32)
    on_f = sb("on_f", [1, 128], F32)
    zc_b = sb("zc_b", [128, 3], BF16)
    mss = sb("mss", [128, NP * 16], BF16)
    mbc = sb("mbc", [16, NP * 128], BF16)
    sel8 = sb("sel8", [8, 8 * 128], BF16)
    selr = sb("sel33b", [33, 2], F32)
    c_col = sb("c_col", [128, 8], F32)
    silc = sb("silc", [128, 8], F32)
    bias_c = sb("bias_c", [128, 8], F32)
    ca1 = sb("ca1", [128, 8], F32)
    sa_c = sb("sa_c", [128, 8], F32)
    ga_c = sb("ga_c", [128, 8], F32)
    cb1 = sb("cb1", [128, 8], F32)
    sb_c = sb("sb_c", [128, 8], F32)
    gb_c = sb("gb_c", [128, 8], F32)
    gsgn = sb("gsgn", [128, 16], F32)
    half_c = sb("half_c", [128, 1], F32)
    probe_s = sb("probe_s", [1, 2], F32)
    ws_c = sb("ws_c", [128, 8], F32)
    wrs_c = sb("wrs_c", [128, 8], F32)
    gws_c = sb("gws_c", [128, 8], F32)
    gbb_c = sb("gbb_c", [128, 8], F32)
    bn6 = sb("bn6", [128, 48], F32)
    bn2 = sb("bn2", [128, 2 * NG], F32)
    spack = sb("spack", [128, 8], F32)
    lrow = sb("lrow", [8, 128], BF16)
    r16 = sb("r16", [16, TT], BF16)
    rowsB = sb("rowsB", [128, TT], F32)
    ss_s = rowsB
    qrow = sb("qrow", [8, 128], BF16)
    qpack = sb("qpack", [128, 8], F32)
    amx = sb("amx", [128, NG], F32)
    negq = sb("negq", [128, 1], F32)
    ws2 = sb("ws2", [128, 8], F32)
    amx2 = sb("amx2", [128, NG], F32)
    fence = sb("fence", [128, 8], F32)

    U0 = cur[0]
    mrow = sb("mrow", [1, 6 * D], F32, at=U0)
    wmq = sb("wmq", [128, 2 * 4096], F32, at=U0 + 24576)
    x_tm = sb("x_tm", [128, NG * D], BF16, at=U0)
    x_fma = sb("x_fma", [128, NC * TT], BF16, at=U0 + 8192)
    h1r = [sb("h1a", [128, NC * TT], BF16, at=U0 + 16384),
           sb("h1b", [128, NC * TT], BF16, at=U0 + 24576)]
    h1T = sb("h1T", [128, NG * D], BF16, at=U0 + 32768)
    kTM = sb("kTM", [128, D], BF16, at=U0 + 40960)
    vTM = sb("vTM", [128, D], BF16, at=U0 + 43008)
    bcA_mu = sb("bcA_mu", [128, TT], F32, at=U0 + 45056)
    bcA_r = sb("bcA_r", [128, TT], F32, at=U0 + 47104)
    o_bf = sb("o_bf", [128, NP * TT], BF16, at=U0)
    osq = sb("osq", [128, TT], BF16, at=U0 + 8192)
    xq_i = sb("xq_i", [128, TT], I32, at=U0 + 9216)
    xq_i2 = sb("xq_i2", [128, TT], I32, at=U0 + 11264)
    x2 = sb("x2", [128, NC * TT], F32, at=U0 + 14336)
    h2 = sb("h2", [128, NC * TT], F32, at=U0 + 30720)
    ouxq = sb("ouxq", [128, NC * TT], BF16, at=U0 + 47104)
    xs_sb = sb("xs_sb", [128, TT], F32, at=U0 + 55296)
    x_fmb = sb("x_fmb", [128, NC * 256], F32, at=U0 + 57344)
    M_sb = sb("M_sb", [128, TT], BF16, at=U0 + 13312)
    UEND = U0 + 57344 + 8192
    wsg1 = sb("wsg1", [128, D], BF16, at=U0 + 57344)
    wqi = sb("wqi", [128, D], I32, at=U0 + 59392)
    wqb = sb("wqb", [128, NC * D], BF16, at=U0 + 14336)
    assert UEND <= CAP, f"union overflow {UEND}"

    pb = [nc.alloc_psum_tensor(f"pb{i}", [128, 512], F32) for i in range(8)]

    sch = Sched(nc)
    sD = sch.sem("sD")
    sG = sch.sem("sG")
    sT = sch.sem("sT")
    sV = sch.sem("sV")
    sS = sch.sem("sS")
    sP = sch.sem("sP")
    sO = sch.sem("sO")
    sWq = sch.sem("sWq")
    sWk = sch.sem("sWk")
    sWv = sch.sem("sWv")
    sWu = sch.sem("sWu")
    sWo = sch.sem("sWo")
    sFW = sch.sem("sFW")
    sXT = sch.sem("sXT")
    sXF = sch.sem("sXF")
    sXB = sch.sem("sXB")
    sGb = sch.sem("sGb")

    def ld(dst_ap, src_ap):
        return lambda e: e.dma_start(out=dst_ap, in_=src_ap)

    for ap_, dr, ev in ((id_b[:], did_b, "l_idb"), (id_f[:], did_f, "l_idf"),
                        (on_f[:], don_f, "l_ones"), (zc_b[:], dzc_b, "l_zcb")):
        sch.op("sync", ld(ap_, dr.ap()), inc=sD, event=ev, inc_n=16)
    sch.op("sync", ld(mss[:], dap(dmss, 0, [[16, 128], [128 * 16, NP], [1, 16]])),
           inc=sD, event="l_mss", inc_n=16)
    sch.op("sync", ld(mbc[:], dap(dmbc, 0, [[128, 16], [16 * 128, NP], [1, 128]])),
           inc=sD, event="l_mbc", inc_n=16)
    sch.op("sync", ld(sel8[:], dsel.ap()), inc=sD, event="l_sel", inc_n=16)
    sch.op("sync", ld(selr[:], dselr.ap()), inc=sD, event="l_selr", inc_n=16)
    sch.op("sync", ld(c_col[:], dc_in.ap()), inc=sD, event="l_c", inc_n=16)
    sch.op("sync", ld(bias_c[:], dfb.ap()), inc=sD, event="l_bias", inc_n=16)
    for _ev in ("l_idb", "l_idf", "l_ones", "l_zcb", "l_mss", "l_mbc",
                "l_sel", "l_selr", "l_c"):
        sch.alias_event(_ev, "l_bias")

    NW = 12

    def wm8(i):
        return ld(wmq[:, (i % 2) * 4096:(i % 2 + 1) * 4096],
                  dap(dwm, i * 512, [[6 * D, 128], [128 * 6 * D, NC], [1, 512]]))

    def wfull(dst, name):
        return ld(dst[:], dap(dws[name], 0, [[D, 128], [128 * D, NC], [1, D]]))

    sch.op("gpsimd", wm8(0), inc=sG, event="l_wm0", inc_n=16)
    sch.op("gpsimd", wm8(1), inc=sGb, event="l_wm1", inc_n=16)
    sch.op("gpsimd", wfull(wqT, "wq_t"), inc=sWq, event="l_wq", inc_n=16)

    def probe0(e):
        e.memset(negq[:], -0.25)
        return e.memset(half_c[:], PROBE)
    sch.op("vector", probe0, inc=sV, event="probe_m")
    sch.op("vector", lambda e: e.tensor_scalar(
               out=half_c[:], in0=half_c[:], scalar1=MAGIC, scalar2=MAGIC,
               op0=ALU.add, op1=ALU.subtract),
           waits=["probe_m"], inc=sV, event="probe_r")
    sch.op("vector", lambda e: e.tensor_scalar_mul(
               out=half_c[:], in0=half_c[:], scalar1=0.5 / PROBE),
           waits=["probe_r"], inc=sV, event="half_rdy")

    sch.op("scalar", lambda e: e.activation(out=silc[:], in_=c_col[:], func=AF.Silu),
           waits=["l_c"], inc=sS, event="silc")

    for i in range(NW):
        def mmod(i=i):
            def f(e):
                ins = None
                for dc in range(NC):
                    ins = e.matmul(pb[7][0:1, 0:512], silc[:, dc:dc + 1],
                                   wmq[:, (i % 2) * 4096 + dc * 512:
                                       (i % 2) * 4096 + dc * 512 + 512],
                                   start=(dc == 0), stop=(dc == NC - 1))
                return ins
            return f
        w = ["silc", f"l_wm{i}"]
        if i > 0:
            w.append(f"e_mod_{i - 1}")
        sch.op("tensor", mmod(), waits=w, inc=sT, event=f"m_mod_{i}")

        def emod(i=i):
            def f(e):
                return e.tensor_copy(out=mrow[:, i * 512:(i + 1) * 512],
                                     in_=pb[7][0:1, 0:512])
            return f
        sch.op("vector", emod(), waits=[f"m_mod_{i}"], inc=sV,
               event=f"e_mod_{i}")
        if i + 2 < NW:
            sch.op("gpsimd", wm8(i + 2), waits=[f"e_mod_{i}"],
                   inc=(sG if i % 2 == 0 else sGb),
                   event=f"l_wm{i + 2}", inc_n=16)
        if i == 2:
            sch.op("gpsimd", wfull(wkT, "wk_t"), inc=sWk, event="l_wk", inc_n=16)
        if i == 5:
            sch.op("gpsimd", wfull(wvT, "wv_t"), inc=sWv, event="l_wv", inc_n=16)
        if i == 8:
            sch.op("gpsimd", wfull(wuT, "wu_t"), inc=sWu, event="l_wu", inc_n=16)

    mod_cols = [sa_c, ca1, ga_c, sb_c, cb1, gb_c]
    for vi, col in enumerate(mod_cols):
        for dc in range(NC):
            seg = vi * D + dc * 128
            need = (seg + 127) // 512

            def tmod(seg=seg, dc=dc):
                def f(e):
                    return e.matmul(pb[6][:, (dc % 4) * 128:(dc % 4) * 128 + 128],
                                    mrow[:1, seg:seg + 128], id_f[:1, :],
                                    start=True, stop=True)
                return f
            w = [f"e_mod_{need}", "l_idf"]
            if vi == 0 and dc == 0:
                w.append("half_rdy")
            if dc % 4 == 0 and (vi or dc):
                w.append(f"ec_{vi}_{dc - 4}" if dc else f"ec_{vi - 1}_4")
            sch.op("tensor", tmod(), waits=w, inc=sT, event=f"tc_{vi}_{dc}")
            if dc % 4 == 3:
                def ecol(col=col, dc=dc):
                    def f(e):
                        return e.tensor_copy(
                            out=col[:, dc - 3:dc + 1],
                            in_=bass.AP(pb[6], 0, [[512, 128], [128, 4]]))
                    return f
                sch.op("vector", ecol(), waits=[f"tc_{vi}_{dc}"], inc=sV,
                       event=f"ec_{vi}_{dc - 3}")

    def fin1(e):
        e.tensor_scalar_add(out=ca1[:], in0=ca1[:], scalar1=1.0)
        return e.tensor_scalar_add(out=cb1[:], in0=cb1[:], scalar1=1.0)
    sch.op("vector", fin1, waits=[f"ec_{vi}_4" for vi in range(6)],
           inc=sV, event="fin1")
    def fin2s(e):
        e.activation(out=gsgn[:, 0:8], in_=ga_c[:], func=AF.Sign)
        e.activation(out=gsgn[:, 8:16], in_=gb_c[:], func=AF.Sign)
    sch.op("scalar", fin2s, waits=["fin1", "half_rdy"], inc=sS, event="fin2s")

    def fin2a(e):
        e.activation(out=ga_c[:], in_=ga_c[:], func=AF.Abs)
        return e.activation(out=gb_c[:], in_=gb_c[:], func=AF.Abs)
    sch.op("scalar", fin2a, waits=["fin2s"], inc=sS, event="fin2a")

    def fin2b(e):
        e.activation(out=ga_c[:], in_=ga_c[:], func=AF.Relu, bias=negq[:, 0:1])
        return e.activation(out=gb_c[:], in_=gb_c[:], func=AF.Relu,
                            bias=negq[:, 0:1])
    sch.op("scalar", fin2b, waits=["fin2a"], inc=sS, event="fin2")
    def fin3(e):
        e.scalar_tensor_tensor(out=ga_c[:], in0=ga_c[:], scalar=0.25,
                               in1=gsgn[:, 0:8], op0=ALU.add, op1=ALU.mult)
        return e.scalar_tensor_tensor(out=gb_c[:], in0=gb_c[:], scalar=0.25,
                                      in1=gsgn[:, 8:16], op0=ALU.add,
                                      op1=ALU.mult)
    sch.op("vector", fin3, waits=["fin2"], inc=sV, event="mod_rdy")

    kv_first = [True, True]

    def emit_prepA(i):
        pe = f"A{i}"
        w = ["mod_rdy"] if i == 0 else [f"A{i-1}_bn"]
        sch.op("gpsimd",
               ld(x_tm[:], dap(dx_tm, i * TT * D, [[D, 128], [128 * D, NG], [1, D]])),
               waits=w, inc=sXT, event=f"{pe}_xtm", inc_n=16)
        w = ["mod_rdy"] if i == 0 else [f"A{i-1}_h1"]
        sch.op("gpsimd",
               ld(x_fma[:], dap(dx_fm, i * TT, [[T, 128], [128 * T, NC], [1, TT]])),
               waits=w, inc=sXF, event=f"{pe}_xfm", inc_n=16)

        def bnst(e):
            for g in range(NG):
                e.bn_stats(out=bn6[:, g * 12:g * 12 + 6],
                           in_=x_tm[:, g * D:g * D + 512])
                e.bn_stats(out=bn6[:, g * 12 + 6:g * 12 + 12],
                           in_=x_tm[:, g * D + 512:(g + 1) * D])
        sch.op("vector", bnst, waits=[f"{pe}_xtm"], inc=sV, event=f"{pe}_bns")

        def bnag(e):
            for g in range(NG):
                e.bn_aggr(out=bn2[:, 2 * g:2 * g + 2],
                          in_=bn6[:, g * 12:g * 12 + 12])
        sch.op("vector", bnag, waits=[f"{pe}_bns"], inc=sV, event=f"{pe}_bn")

        sch.op("vector", lambda e: e.tensor_scalar_add(
                   out=bn2[:, 1:8:2], in0=bn2[:, 1:8:2], scalar1=1e-5),
               waits=[f"{pe}_bn"], inc=sV, event=f"{pe}_r0a")
        sch.op("vector", lambda e: e.reciprocal(out=bn2[:, 1:8:2],
                                                in_=bn2[:, 1:8:2]),
               waits=[f"{pe}_r0a"], inc=sV, event=f"{pe}_r0")
        sch.op("scalar", lambda e: e.activation(out=spack[:, 1:8:2],
                                                in_=bn2[:, 1:8:2], func=AF.Sqrt),
               waits=[f"{pe}_r0"], inc=sS, event=f"{pe}_r1")
        sch.op("vector", lambda e: e.tensor_mul(out=spack[:, 0:8:2],
                                                in0=bn2[:, 0:8:2],
                                                in1=spack[:, 1:8:2]),
               waits=[f"{pe}_r1"], inc=sV, event=f"{pe}_r2")
        w = [f"{pe}_r2", "l_idf"]
        if i:
            w.append(f"A{i-1}_bcmu")
        sch.op("tensor", lambda e: e.matmul(pb[5][0:8, 0:128], spack[:, 0:8],
                                            id_f[:], start=True, stop=True),
               waits=w, inc=sT, event=f"{pe}_pk")
        sch.op("vector", lambda e: e.tensor_copy(out=lrow[:],
                                                 in_=pb[5][0:8, 0:128]),
               waits=[f"{pe}_pk"], inc=sV, event=f"{pe}_lrow")

        def bc0(e):
            ins = None
            for g in range(NG):
                ins = e.matmul(pb[5][:, g * 128:(g + 1) * 128],
                               sel8[:, (2 * g) * 128:(2 * g + 1) * 128],
                               lrow[:], start=True, stop=True)
            return ins
        sch.op("tensor", bc0, waits=[f"{pe}_lrow", "l_sel"], inc=sT, event=f"{pe}_b0")

        def bc1(e):
            ins = None
            for g in range(NG):
                ins = e.matmul(pb[6][:, g * 128:(g + 1) * 128],
                               sel8[:, (2 * g + 1) * 128:(2 * g + 2) * 128],
                               lrow[:], start=True, stop=True)
            return ins
        w = [f"{pe}_lrow"]
        w.append("half_rdy" if i == 0 else f"A{i-1}_bcr")
        sch.op("tensor", bc1, waits=w, inc=sT, event=f"{pe}_b1")
        sch.op("vector", lambda e: e.tensor_copy(out=bcA_mu[:], in_=pb[5][:]),
               waits=[f"{pe}_b0"], inc=sV, event=f"{pe}_bcmu")
        sch.op("vector", lambda e: e.tensor_copy(out=bcA_r[:], in_=pb[6][:]),
               waits=[f"{pe}_b1"], inc=sV, event=f"{pe}_bcr")

        def xhat_m(e):
            ins = None
            for dc in range(NC):
                sl = slice(dc * TT, (dc + 1) * TT)
                ins = e.tensor_mul(out=x_fma[:, sl], in0=x_fma[:, sl],
                                   in1=bcA_r[:])
            return ins
        sch.op("vector", xhat_m, waits=[f"{pe}_bcmu", f"{pe}_bcr", f"{pe}_xfm"],
               inc=sV, event=f"{pe}_xhm")

        def xhat_s(e):
            ins = None
            for dc in range(NC):
                sl = slice(dc * TT, (dc + 1) * TT)
                ins = e.tensor_sub(out=x_fma[:, sl], in0=x_fma[:, sl],
                                   in1=bcA_mu[:])
            return ins
        sch.op("vector", xhat_s, waits=[f"{pe}_xhm"], inc=sV,
               event=f"{pe}_xhat")

        def h1f(e):
            ins = None
            for dc in range(NC):
                sl = slice(dc * TT, (dc + 1) * TT)
                ins = e.activation(out=h1r[i % 2][:, sl], in_=x_fma[:, sl],
                                   func=AF.Identity, scale=ca1[:, dc:dc + 1],
                                   bias=sa_c[:, dc:dc + 1])
            return ins
        w = [f"{pe}_xhat"]
        if i >= 2:
            w.append(f"A{i-2}_mmdone")
        sch.op("scalar", h1f, waits=w, inc=sS, event=f"{pe}_h1")

    def emit_mmA(i):
        pe = f"A{i}"
        hb = h1r[i % 2]
        for pi, (wt, dst, wev, fn, sc) in enumerate(
                ((wqT, q_sb, "l_wq", AF.Relu, 0.125),
                 (wuT, u_sb, "l_wu", AF.Silu, 1.0))):
            for ec in range(NC):
                bank = pb[2 + ec % 2]

                def mm(wt=wt, ec=ec, bank=bank, hb=hb):
                    def f(e):
                        ins = None
                        for dc in range(NC):
                            ins = e.matmul(
                                bank[:],
                                wt[:, dc * D + ec * 128:dc * D + ec * 128 + 128],
                                hb[:, dc * TT:(dc + 1) * TT],
                                start=(dc == 0), stop=(dc == NC - 1))
                        return ins
                    return f
                w = [f"{pe}_h1", wev]
                gi = pi * NC + ec
                if gi >= 2:
                    pj, ej = divmod(gi - 2, NC)
                    w.append(f"{pe}_pe{pj}_{ej}")
                elif i:
                    w.append(f"A{i-1}_kve{NG-1}_{2 + ec}")
                sch.op("tensor", mm(), waits=w, inc=sT, event=f"{pe}_p{pi}_{ec}")

                def ev(ec=ec, bank=bank, dst=dst, fn=fn, sc=sc, i=i):
                    def f(e):
                        return e.activation(
                            out=dst[:, ec * T + i * TT:ec * T + i * TT + TT],
                            in_=bank[:], func=fn, scale=sc)
                    return f
                sch.op("scalar", ev(), waits=[f"{pe}_p{pi}_{ec}"], inc=sS,
                       event=f"{pe}_pe{pi}_{ec}")
        for g in range(NG):
            for pi, (wt, dst, wev, fn, sc) in enumerate(
                    ((wkT, kTM, "l_wk", AF.Relu, 0.125),
                     (wvT, vTM, "l_wv", AF.Identity, 1.0))):
                for hf in range(2):
                    idx = g * 4 + pi * 2 + hf
                    bank = pb[2 + idx % 2]

                    def mm(wt=wt, g=g, hf=hf, bank=bank, hb=hb):
                        def f(e):
                            ins = None
                            for dc in range(NC):
                                ins = e.matmul(
                                    bank[:],
                                    hb[:, dc * TT + g * 128:dc * TT + g * 128 + 128],
                                    wt[:, dc * D + hf * 512:dc * D + (hf + 1) * 512],
                                    start=(dc == 0), stop=(dc == NC - 1))
                            return ins
                        return f
                    w = [f"{pe}_h1", wev]
                    if idx >= 2:
                        pg, r = divmod(idx - 2, 4)
                        w.append(f"{pe}_kve{pg}_{r}")
                    else:
                        w.append(f"{pe}_pe1_{NC - 2 + idx}")
                    if pi == 0 and hf == 0 and g:
                        w.append(f"{pe}_kv{g-1}")
                    sch.op("tensor", mm(), waits=w, inc=sT,
                           event=f"{pe}_kvm{g}_{pi * 2 + hf}")

                    def ev(hf=hf, bank=bank, dst=dst, fn=fn, sc=sc):
                        def f(e):
                            return e.activation(out=dst[:, hf * 512:(hf + 1) * 512],
                                                in_=bank[:], func=fn, scale=sc)
                        return f
                    sch.op("scalar", ev(), waits=[f"{pe}_kvm{g}_{pi * 2 + hf}"],
                           inc=sS, event=f"{pe}_kve{g}_{pi * 2 + hf}")

            def kv(g=g, i=i):
                def f(e):
                    ins = None
                    for p in range(NP):
                        bank = pb[p // 4]
                        ins = e.matmul(bank[:, (p % 4) * 128:(p % 4 + 1) * 128],
                                       kTM[:, p * 128:(p + 1) * 128],
                                       vTM[:, p * 128:(p + 1) * 128],
                                       start=(i == 0 and g == 0 and p % 4 == 0),
                                       stop=(i == NT - 1 and g == NG - 1 and p % 4 == 3))
                    return ins
                return f
            sch.op("tensor", kv(), waits=[f"{pe}_kve{g}_1", f"{pe}_kve{g}_3"],
                   inc=sT, event=f"{pe}_kv{g}")
        sch.op("tensor", lambda e: None,
               waits=[f"{pe}_kv{NG-1}", f"{pe}_p1_{NC-1}"],
               inc=sT, event=f"{pe}_mmdone")

    emit_prepA(0)
    for i in range(NT):
        if i + 1 < NT:
            emit_prepA(i + 1)
        emit_mmA(i)

    A_END = f"A{NT-1}_mmdone"
    sch.op("sync", ld(fwS[:], dap(dws["ffn_w"], 0, [[D, 128], [128 * D, NC], [1, D]])),
           waits=[A_END], inc=sFW, event="l_fw", inc_n=16)
    sch.op("gpsimd", wfull(woT, "wo_t"), waits=[A_END], inc=sWo,
           event="l_wo", inc_n=16)

    def famax(e):
        ins = None
        for ec in range(NC):
            ins = e.tensor_reduce(out=ws_c[:, ec:ec + 1],
                                  in_=fwS[:, ec * D:(ec + 1) * D],
                                  axis=AX, op=ALU.max,
                                  apply_absolute_value=True)
        return ins
    sch.op("vector", famax, waits=["l_fw"], inc=sV, event="f_amax")
    sch.op("gpsimd", lambda e: e.tensor_scalar_max(out=ws2[:], in0=ws_c[:],
                                                   scalar1=1e-8),
           waits=["f_amax"], inc=sP, event="f_clamp")

    sch.op("scalar", lambda e: e.activation(out=wrs_c[:], in_=ws2[:],
                                            func=AF.Ln, scale=1.0 / 127.0),
           waits=["f_clamp"], inc=sS, event="f_rsa")

    def f_rs(e):
        e.activation(out=wrs_c[:], in_=wrs_c[:], func=AF.Exp, scale=-1.0)
        return e.activation(out=ws_c[:], in_=ws2[:], func=AF.Identity,
                            scale=1.0 / 127.0)
    sch.op("scalar", f_rs, waits=["f_rsa"], inc=sS, event="f_rs")

    def fscale(e):
        e.tensor_mul(out=gws_c[:], in0=gb_c[:], in1=ws_c[:])
        return e.tensor_mul(out=gbb_c[:], in0=gb_c[:], in1=bias_c[:])
    sch.op("vector", fscale, waits=["f_rs", "mod_rdy", "l_bias"], inc=sV,
           event="f_scale")
    for ec in range(NC):
        def sgn1(ec=ec):
            def f(e):
                return e.activation(out=wsg1[:], in_=fwS[:, ec * D:(ec + 1) * D],
                                    func=AF.Sign)
            return f
        w = ["l_fw", A_END]
        if ec:
            w.append(f"f_wqv{ec - 1}")
        sch.op("scalar", sgn1(), waits=w, inc=sS, event=f"f_sg{ec}")

        def wq1a(ec=ec):
            def f(e):
                sl = slice(ec * D, (ec + 1) * D)
                return e.tensor_scalar(out=fwS[:, sl], in0=fwS[:, sl],
                                       scalar1=wrs_c[:, ec:ec + 1],
                                       scalar2=None, op0=ALU.mult)
            return f
        w = ["f_rs", f"f_sg{ec}"]
        if ec:
            w.append(f"f_wq{ec - 1}")
        sch.op("vector", wq1a(), waits=w, inc=sV, event=f"f_wqa{ec}")

        def wq1b(ec=ec):
            def f(e):
                return e.scalar_tensor_tensor(out=wqi[:], in0=wsg1[:],
                                              scalar=half_c[:, 0:1],
                                              in1=fwS[:, ec * D:(ec + 1) * D],
                                              op0=ALU.mult, op1=ALU.add)
            return f
        sch.op("vector", wq1b(), waits=[f"f_wqa{ec}", "half_rdy"],
               inc=sV, event=f"f_wqv{ec}")

        def wq2(ec=ec):
            def f(e):
                return e.copy(out=wqb[:, ec * D:(ec + 1) * D], in_=wqi[:])
            return f
        sch.op("scalar", wq2(), waits=[f"f_wqv{ec}"], inc=sS, event=f"f_wq{ec}")
    _tq_idxs = []
    for dcb in range(NC):
        for hf in range(2):
            def tq(dcb=dcb, hf=hf):
                def f(e):
                    ins = None
                    for eq in range(4):
                        ec = hf * 4 + eq
                        ins = e.matmul(
                            pb[7][:, eq * 128:(eq + 1) * 128],
                            wqb[:, ec * D + dcb * 128:ec * D + dcb * 128 + 128],
                            id_b[:], start=True, stop=True)
                    return ins
                return f
            w = [f"f_wq{hf * 4 + 3}", A_END]
            ti = dcb * 2 + hf
            if ti:
                w.append(f"f_te{(ti - 1) // 2}_{(ti - 1) % 2}")
            _tq_idxs.append(len(sch.ops))
            sch.op("tensor", tq(), waits=w, inc=sT, event=f"f_t{dcb}_{hf}")

            def tqe(dcb=dcb, hf=hf):
                def f(e):
                    return e.copy(
                        out=fqT[:, dcb * D + hf * 512:dcb * D + (hf + 1) * 512],
                        in_=pb[7][:])
                return f
            _tq_idxs.append(len(sch.ops))
            sch.op("scalar", tqe(), waits=[f"f_t{dcb}_{hf}"], inc=sS,
                   event=f"f_te{dcb}_{hf}")
    F_END = f"f_te{NC-1}_1"

    _kvz_idx = len(sch.ops)
    sch.op("gpsimd", lambda e: e.memset(kv_sb[:], 0.0), inc=sP, event="kvz")
    sch.op("gpsimd", lambda e: e.memset(rowsB[:], 0.0), waits=[A_END],
           inc=sP, event="rbz")

    def kvev(e):
        ins = None
        for p in range(NP):
            bank = pb[p // 4]
            c0 = (p % 4) * 128
            ins = e.tensor_copy(out=kv_sb[0:64, p * 128:p * 128 + 64],
                                in_=bank[0:64, c0:c0 + 64])
            ins = e.tensor_copy(out=kv_sb[64:128, p * 128 + 64:(p + 1) * 128],
                                in_=bank[64:128, c0 + 64:c0 + 128])
        return ins
    _kvev_idx = len(sch.ops)
    sch.op("vector", kvev, waits=[A_END, "kvz"], inc=sV, event="kv_rdy")
    _b0f = []

    _P_B0 = float(len(sch.ops))
    for i in range(NT):
        pe = f"B{i}"
        for p in range(NP):
            def om(p=p, i=i):
                def f(e):
                    return e.matmul(pb[p % 2][:], kv_sb[:, p * 128:(p + 1) * 128],
                                    q_sb[:, p * T + i * TT:p * T + i * TT + TT],
                                    start=True, stop=True)
                return f
            w = ["kv_rdy"]
            if p >= 2:
                w.append(f"{pe}_oe{p - 2}")
            elif i:
                w += [f"B{i-1}_oe{NP - 2 + p}", f"B{i-1}_ou{NP - 1}"]
            if i == 0:
                _b0f.append(len(sch.ops))
            sch.op("tensor", om(), waits=w, inc=sT, event=f"{pe}_o{p}")

            def oev(p=p):
                def f(e):
                    e.activation(out=o_bf[:, p * TT:(p + 1) * TT],
                                 in_=pb[p % 2][:], func=AF.Identity)
                    return e.activation(out=osq[:], in_=pb[p % 2][:], func=AF.Square)
                return f
            w = [f"{pe}_o{p}"]
            if p:
                w.append(f"{pe}_ss{p - 1}")
            if p == 0 and i == 0:
                w.append(A_END)
            if i == 0:
                _b0f.append(len(sch.ops))
            sch.op("scalar", oev(), waits=w, inc=sS, event=f"{pe}_oe{p}")

            def ssm(p=p):
                def f(e):
                    return e.matmul(pb[2][0:16, :], mss[:, p * 16:(p + 1) * 16],
                                    osq[:], start=(p == 0), stop=(p == NP - 1))
                return f
            w = [f"{pe}_oe{p}", "l_mss"]
            if p == 0 and i:
                w.append(f"B{i-1}_sse")
            if p == 0 and i == 0:
                w.append(f"A{NT-1}_kve{NG-1}_2")
            if i == 0:
                _b0f.append(len(sch.ops))
            sch.op("tensor", ssm(), waits=w, inc=sT, event=f"{pe}_ss{p}")
        w = [f"{pe}_ss{NP-1}", "rbz"]
        if i:
            w.append(f"B{i-1}_lb1")
            w.append(f"B{i-1}_lb0")
        if i == 0:
            _b0f.append(len(sch.ops))
        sch.op("vector", lambda e, w=w: e.tensor_copy(out=ss_s[0:16, :], in_=pb[2][0:16, :]),
               waits=w, inc=sV, event=f"{pe}_sse")
        if i == 0:
            _b0f.append(len(sch.ops))
        sch.op("scalar", lambda e: e.activation(out=ss_s[0:16, :], in_=ss_s[0:16, :], func=AF.Sqrt),
               waits=[f"{pe}_sse"], inc=sS, event=f"{pe}_sq")

        if i == 0:
            _b0f.append(len(sch.ops))
        sch.op("vector", lambda e: e.tensor_scalar_max(
                   out=ss_s[0:16, :], in0=ss_s[0:16, :], scalar1=8e-12),
               waits=[f"{pe}_sq"], inc=sV, event=f"{pe}_sr0")
        if i == 0:
            _b0f.append(len(sch.ops))
        sch.op("vector", lambda e: e.reciprocal(out=ss_s[0:16, :],
                                                in_=ss_s[0:16, :]),
               waits=[f"{pe}_sr0"], inc=sV, event=f"{pe}_sr1")
        if i == 0:
            _b0f.append(len(sch.ops))
        sch.op("vector", lambda e: e.tensor_scalar(
                   out=r16[:], in0=ss_s[0:16, :], scalar1=8.0,
                   scalar2=None, op0=ALU.mult),
               waits=[f"{pe}_sr1"], inc=sV, event=f"{pe}_r16")
        for p in range(NP):
            bank = pb[3 if p % 2 == 0 else 6]

            def rbm(p=p, bank=bank):
                def f(e):
                    return e.matmul(bank[:], mbc[:, p * 128:(p + 1) * 128],
                                    r16[:], start=True, stop=True)
                return f
            w = [f"{pe}_r16", "l_mbc"]
            if p >= 2:
                w.append(f"{pe}_ou{p - 2}")
            elif i:
                w.append(f"B{i-1}_xqv{NC - 1}" if p == 0 else f"B{i-1}_xse")
            if i == 0:
                _b0f.append(len(sch.ops))
            sch.op("tensor", rbm(), waits=w, inc=sT, event=f"{pe}_rb{p}")

            def ouf1(p=p, bank=bank):
                def f(e):
                    return e.tensor_mul(out=ouxq[:, p * TT:(p + 1) * TT],
                                        in0=o_bf[:, p * TT:(p + 1) * TT],
                                        in1=bank[:])
                return f
            w = [f"{pe}_rb{p}", f"{pe}_oe{p}"]
            if p == 0 and i > 0:
                w.append(f"B{i-1}_xqc{NC - 1}")
            if i == 0:
                _b0f.append(len(sch.ops))
            sch.op("vector", ouf1(), waits=w, inc=sV, event=f"{pe}_oa{p}")

            def ouf2(p=p, i=i):
                def f(e):
                    return e.tensor_mul(
                        out=ouxq[:, p * TT:(p + 1) * TT],
                        in0=ouxq[:, p * TT:(p + 1) * TT],
                        in1=u_sb[:, p * T + i * TT:p * T + i * TT + TT])
                return f
            if i == 0:
                _b0f.append(len(sch.ops))
            sch.op("vector", ouf2(), waits=[f"{pe}_oa{p}"], inc=sV,
                   event=f"{pe}_ou{p}")
        for ec in range(NC):
            bank = pb[4 + ec % 2]

            def wom(ec=ec, bank=bank):
                def f(e):
                    ins = None
                    for dc in range(NC):
                        ins = e.matmul(
                            bank[:],
                            woT[:, dc * D + ec * 128:dc * D + ec * 128 + 128],
                            ouxq[:, dc * TT:(dc + 1) * TT],
                            start=(dc == 0), stop=(dc == NC - 1))
                    return ins
                return f
            w = [f"{pe}_ou{NP - 1}", "l_wo"]
            if ec >= 2:
                w.append(f"{pe}_x2c{ec - 2}")
            elif i:
                w.append(f"B{i-1}_ff{NC - 2 + ec}")
            elif ec != 0:
                w.append(f"A{NT-1}_bcmu")
            if i == 0:
                _b0f.append(len(sch.ops))
            sch.op("tensor", wom(), waits=w, inc=sT, event=f"{pe}_wo{ec}")

            def x2f(ec=ec, bank=bank):
                def f(e):
                    return e.tensor_scalar(out=x2[:, ec * TT:(ec + 1) * TT],
                                           in0=bank[:], scalar1=ga_c[:, ec:ec + 1],
                                           scalar2=None, op0=ALU.mult)
                return f
            w = [f"{pe}_wo{ec}"]
            if ec == 0:
                w.append(F_END if i == 0 else f"B{i-1}_x3{NC-1}")
            sch.op("vector", x2f(), waits=w, inc=sV, event=f"{pe}_x2c{ec}")
        for hf in range(2):
            w = []
            if i == 0 and hf == 0:
                w.append(f"f_wq{NC - 1}")
            if i or hf:
                w.append(f"{pe}_xa{hf-1}" if hf else f"B{i-1}_xa1")
            if hf == 0 and i:
                w.append(f"B{i-1}_xa1")
            sch.op("sync", ld(x_fmb[:], dap(dx_fm, i * TT + hf * 256,
                                            [[T, 128], [128 * T, NC], [1, 256]])),
                   waits=w, inc=sXB, event=f"{pe}_xf{hf}", inc_n=16)

            def xadd(hf=hf):
                def f(e):
                    ins = None
                    for ec in range(NC):
                        sl = slice(ec * TT + hf * 256, ec * TT + hf * 256 + 256)
                        ins = e.tensor_add(out=x2[:, sl], in0=x2[:, sl],
                                           in1=x_fmb[:, ec * 256:(ec + 1) * 256])
                    return ins
                return f
            sch.op("vector", xadd(), waits=[f"{pe}_xf{hf}", f"{pe}_x2c{NC-1}"],
                   inc=sV, event=f"{pe}_xa{hf}")
        def x2bf(e):
            return e.activation(out=ouxq[:], in_=x2[:], func=AF.Identity)
        sch.op("scalar", x2bf, waits=[f"{pe}_xa1", f"{pe}_wo{NC-1}"],
               inc=sS, event=f"{pe}_x2b")

        def x2sq(e):
            return e.tensor_mul(out=o_bf[:], in0=x2[:], in1=x2[:])
        sch.op("vector", x2sq, waits=[f"{pe}_xa1", f"{pe}_ou{NP-1}"],
               inc=sV, event=f"{pe}_x2q")

        def lnm(e):
            ins = None
            for dc in range(NC):
                ins = e.matmul(pb[7][0:1, :], zc_b[:, 0:1],
                               ouxq[:, dc * TT:(dc + 1) * TT],
                               start=(dc == 0), stop=(dc == NC - 1))
            for dc in range(NC):
                ins = e.matmul(pb[6][0:1, :], zc_b[:, 0:1],
                               o_bf[:, dc * TT:(dc + 1) * TT],
                               start=(dc == 0), stop=(dc == NC - 1))
            return ins
        w = [f"{pe}_x2b", f"{pe}_x2q", "l_zcb", f"{pe}_ss{NP-1}",
             f"{pe}_ou{NP-1}"]
        if i:
            w.append(f"B{i-1}_qr")
        else:
            w.append(F_END)
        sch.op("tensor", lnm, waits=w, inc=sT, event=f"{pe}_l2m")

        def l2ev(e):
            e.tensor_copy(out=rowsB[0:1, :], in_=pb[7][0:1, :])
            return e.tensor_copy(out=rowsB[32:33, :], in_=pb[6][0:1, :])
        sch.op("vector", l2ev, waits=[f"{pe}_l2m"], inc=sV, event=f"{pe}_l2ev")

        def l2g(e):
            ins = None
            for g in range(NG):
                ins = e.matmul(pb[7][:, 8 + g * 2:8 + g * 2 + 2],
                               rowsB[0:33, g * 128:(g + 1) * 128], selr[:],
                               start=True, stop=True)
            return ins
        sch.op("tensor", l2g, waits=[f"{pe}_l2ev", "l_selr"], inc=sT,
               event=f"{pe}_l2g")

        sch.op("vector", lambda e: e.tensor_scalar_mul(
                   out=bn2[:], in0=pb[7][:, 8:16], scalar1=1.0 / D),
               waits=[f"{pe}_l2g"], inc=sV, event=f"{pe}_l2ca")
        sch.op("vector", lambda e: e.tensor_mul(
                   out=spack[:, 0:8:2], in0=bn2[:, 0:8:2], in1=bn2[:, 0:8:2]),
               waits=[f"{pe}_l2ca"], inc=sV, event=f"{pe}_l2cb")
        sch.op("vector", lambda e: e.tensor_sub(
                   out=bn2[:, 1:8:2], in0=bn2[:, 1:8:2], in1=spack[:, 0:8:2]),
               waits=[f"{pe}_l2cb"], inc=sV, event=f"{pe}_l2cc")
        sch.op("vector", lambda e: e.tensor_scalar_add(
                   out=bn2[:, 1:8:2], in0=bn2[:, 1:8:2], scalar1=1e-5),
               waits=[f"{pe}_l2cc"], inc=sV, event=f"{pe}_l2cd")
        sch.op("vector", lambda e: e.reciprocal(out=bn2[:, 1:8:2],
                                                in_=bn2[:, 1:8:2]),
               waits=[f"{pe}_l2cd"], inc=sV, event=f"{pe}_l2c")
        sch.op("scalar", lambda e: e.activation(out=spack[:, 1:8:2],
                                                in_=bn2[:, 1:8:2], func=AF.Sqrt),
               waits=[f"{pe}_l2c"], inc=sS, event=f"{pe}_l2s")
        sch.op("vector", lambda e: e.tensor_mul(out=spack[:, 0:8:2],
                                                in0=bn2[:, 0:8:2],
                                                in1=spack[:, 1:8:2]),
               waits=[f"{pe}_l2s"], inc=sV, event=f"{pe}_l2c2")
        sch.op("tensor", lambda e: e.matmul(pb[7][0:8, 0:128], spack[:, 0:8],
                                            id_f[:], start=True, stop=True),
               waits=[f"{pe}_l2c2"], inc=sT, event=f"{pe}_l2pk")
        sch.op("vector", lambda e: e.tensor_copy(out=lrow[:],
                                                 in_=pb[7][0:8, 0:128]),
               waits=[f"{pe}_l2pk"], inc=sV, event=f"{pe}_l2e")
        def l2b0(e):
            ins = None
            for g in range(NG):
                ins = e.matmul(pb[3][:, g * 128:(g + 1) * 128],
                               sel8[:, (2 * g + 1) * 128:(2 * g + 2) * 128],
                               lrow[:], start=True, stop=True)
            return ins
        sch.op("tensor", l2b0, waits=[f"{pe}_l2e", f"{pe}_ou{NP-1}", "l_selr"],
               inc=sT, event=f"{pe}_lb0")

        def l2b1(e):
            ins = None
            for g in range(NG):
                ins = e.matmul(pb[6][:, g * 128:(g + 1) * 128],
                               sel8[:, (2 * g) * 128:(2 * g + 1) * 128],
                               lrow[:], start=True, stop=True)
            return ins
        sch.op("tensor", l2b1, waits=[f"{pe}_l2e", f"{pe}_ou{NP-1}"], inc=sT,
               event=f"{pe}_lb1")
        def xh2m(e):
            for dc in range(NC):
                sl = slice(dc * TT, (dc + 1) * TT)
                e.tensor_mul(out=h2[:, sl], in0=x2[:, sl], in1=pb[3][:])
        w = [f"{pe}_lb0", f"{pe}_lb1", f"{pe}_xa1"]
        w.append(F_END if i == 0 else f"B{i-1}_outd")
        sch.op("vector", xh2m, waits=w, inc=sV, event=f"{pe}_xh2m")

        def xh2s(e):
            for dc in range(NC):
                sl = slice(dc * TT, (dc + 1) * TT)
                e.tensor_sub(out=h2[:, sl], in0=h2[:, sl], in1=pb[6][:])
        sch.op("vector", xh2s, waits=[f"{pe}_xh2m"], inc=sV, event=f"{pe}_xh2")

        def h2f(e):
            ins = None
            for dc in range(NC):
                sl = slice(dc * TT, (dc + 1) * TT)
                ins = e.activation(out=h2[:, sl], in_=h2[:, sl], func=AF.Identity,
                                   scale=cb1[:, dc:dc + 1], bias=sb_c[:, dc:dc + 1])
            return ins
        sch.op("scalar", h2f, waits=[f"{pe}_xh2"], inc=sS, event=f"{pe}_h2")
        def hamax(e):
            return e.tensor_reduce(
                out=M_sb[:, 0:TT],
                in_=h2[:].rearrange("p (c t) -> p t c", c=NC),
                axis=AX, op=ALU.max, apply_absolute_value=True)
        sch.op("vector", hamax, waits=[f"{pe}_h2", f"{pe}_xa1"], inc=sV,
               event=f"{pe}_amx")
        def mtr(e):
            ins = None
            for g in range(NG):
                ins = e.matmul(pb[7][:, g * 128:(g + 1) * 128],
                               M_sb[:, g * 128:(g + 1) * 128], id_b[:],
                               start=True, stop=True)
            return ins
        sch.op("tensor", mtr, waits=[f"{pe}_amx", f"{pe}_l2e"], inc=sT,
               event=f"{pe}_mtr")

        def ared(e):
            ins = None
            for g in range(NG):
                ins = e.tensor_reduce(out=amx[:, g:g + 1],
                                      in_=pb[7][:, g * 128:(g + 1) * 128],
                                      axis=AX, op=ALU.max)
            return ins
        w = [f"{pe}_mtr"]
        if i:
            w.append(f"B{i-1}_am2")
        sch.op("vector", ared, waits=w, inc=sV, event=f"{pe}_am")
        w = [f"{pe}_am"]
        if i:
            w.append(f"B{i-1}_qs")
        sch.op("gpsimd", lambda e: e.tensor_scalar_max(out=amx2[:], in0=amx[:],
                                                       scalar1=1e-8),
               waits=w, inc=sP, event=f"{pe}_am2")

        w = [f"{pe}_am2"]
        if i:
            w.append(f"B{i-1}_qt")
        sch.op("scalar", lambda e: e.activation(out=qpack[:, 0:4],
                                                in_=amx2[:], func=AF.Ln,
                                                scale=1.0 / 127.0),
               waits=w, inc=sS, event=f"{pe}_qsa")

        def qsc(e):
            e.activation(out=qpack[:, 0:4], in_=qpack[:, 0:4], func=AF.Exp,
                         scale=-1.0)
            return e.activation(out=qpack[:, 4:8], in_=amx2[:],
                                func=AF.Identity, scale=1.0 / 127.0)
        sch.op("scalar", qsc, waits=[f"{pe}_qsa"], inc=sS, event=f"{pe}_qs")
        sch.op("tensor", lambda e: e.matmul(pb[7][0:8, 0:128], qpack[:, 0:8],
                                            id_f[:], start=True, stop=True),
               waits=[f"{pe}_qs"], inc=sT, event=f"{pe}_qt")
        sch.op("scalar", lambda e: e.copy(out=qrow[:], in_=pb[7][0:8, 0:128]),
               waits=[f"{pe}_qt"], inc=sS, event=f"{pe}_qr")
        def qb0(e):
            ins = None
            for g in range(NG):
                ins = e.matmul(pb[3][:, g * 128:(g + 1) * 128],
                               sel8[:, g * 128:(g + 1) * 128],
                               qrow[:], start=True, stop=True)
            return ins
        sch.op("tensor", qb0, waits=[f"{pe}_qr", f"{pe}_xh2"], inc=sT,
               event=f"{pe}_qb0")

        def qb1(e):
            ins = None
            for g in range(NG):
                ins = e.matmul(pb[6][:, g * 128:(g + 1) * 128],
                               sel8[:, (4 + g) * 128:(5 + g) * 128],
                               qrow[:], start=True, stop=True)
            return ins
        sch.op("tensor", qb1, waits=[f"{pe}_qr", f"{pe}_xh2"], inc=sT,
               event=f"{pe}_qb1")
        w = [f"{pe}_qb1"]
        if i == 0:
            w.append(F_END)
        else:
            w.append(f"B{i-1}_ffe{NC-1}")
        sch.op("scalar", lambda e: e.copy(out=xs_sb[:], in_=pb[6][:]),
               waits=w, inc=sS, event=f"{pe}_xse")
        sch.op("scalar", lambda e: e.activation(out=o_bf[:], in_=h2[:],
                                                func=AF.Sign),
               waits=[f"{pe}_h2", f"{pe}_l2m"], inc=sS, event=f"{pe}_xqs")
        for dc in range(NC):
            def xqv(dc=dc):
                def f(e):
                    return e.tensor_mul(out=h2[:, dc * TT:(dc + 1) * TT],
                                        in0=h2[:, dc * TT:(dc + 1) * TT], in1=pb[3][:])
                return f
            w = [f"{pe}_qb0", f"{pe}_h2", f"{pe}_xqs"]
            sch.op("vector", xqv(), waits=w, inc=sV, event=f"{pe}_xqv{dc}")

            def xqr(dc=dc):
                def f(e):
                    return e.scalar_tensor_tensor(
                        out=(xq_i if dc % 2 == 0 else xq_i2)[:],
                        in0=o_bf[:, dc * TT:(dc + 1) * TT],
                        scalar=half_c[:, 0:1],
                        in1=h2[:, dc * TT:(dc + 1) * TT],
                        op0=ALU.mult, op1=ALU.add)
                return f
            w = [f"{pe}_xqv{dc}", f"{pe}_xqs"]
            if dc >= 2:
                w.append(f"{pe}_xqc{dc - 2}")
            sch.op("vector", xqr(), waits=w, inc=sV, event=f"{pe}_xqi{dc}")

            def xqc2(dc=dc):
                def f(e):
                    return e.copy(out=ouxq[:, dc * TT:(dc + 1) * TT],
                                  in_=(xq_i if dc % 2 == 0 else xq_i2)[:])
                return f
            w = [f"{pe}_xqi{dc}", f"{pe}_l2m"]
            sch.op("scalar", xqc2(), waits=w, inc=sS, event=f"{pe}_xqc{dc}")
        def x2p(e):
            ins = None
            for ec in range(NC):
                ins = e.activation(out=x2[:, ec * TT:(ec + 1) * TT],
                                   in_=x2[:, ec * TT:(ec + 1) * TT],
                                   func=AF.Identity, bias=gbb_c[:, ec:ec + 1])
            return ins
        sch.op("scalar", x2p, waits=[f"{pe}_xh2", f"{pe}_x2q", "f_scale"], inc=sS,
               event=f"{pe}_x2p")
        for ec in range(NC):
            bank = pb[4 + ec % 2]

            def ffm(ec=ec, bank=bank):
                def f(e):
                    ins = None
                    for dc in range(NC):
                        ins = e.matmul(
                            bank[:],
                            fqT[:, dc * D + ec * 128:dc * D + ec * 128 + 128],
                            ouxq[:, dc * TT:(dc + 1) * TT],
                            start=(dc == 0), stop=(dc == NC - 1))
                    return ins
                return f
            w = [f"{pe}_xqc{NC-1}"]
            if ec >= 2:
                w.append(f"{pe}_ffe{ec - 2}")
            else:
                w.append(f"{pe}_x2c{NC - 1}")
            if i == 0:
                w.append(F_END)
            sch.op("tensor", ffm(), waits=w, inc=sT, event=f"{pe}_ff{ec}")

            def ffev(ec=ec, bank=bank):
                def f(e):
                    return e.scalar_tensor_tensor(
                        out=h2[:, ec * TT:(ec + 1) * TT], in0=bank[:],
                        scalar=gws_c[:, ec:ec + 1], in1=xs_sb[:],
                        op0=ALU.mult, op1=ALU.mult)
                return f
            sch.op("vector", ffev(), waits=[f"{pe}_ff{ec}", f"{pe}_xse", "f_scale",
                                            f"{pe}_xqv{ec}"],
                   inc=sV, event=f"{pe}_ffe{ec}")

            def x3f(ec=ec):
                def f(e):
                    return e.tensor_add(out=h2[:, ec * TT:(ec + 1) * TT],
                                        in0=h2[:, ec * TT:(ec + 1) * TT],
                                        in1=x2[:, ec * TT:(ec + 1) * TT])
                return f
            sch.op("gpsimd", x3f(), waits=[f"{pe}_ffe{ec}", f"{pe}_x2p"],
                   inc=sP, event=f"{pe}_x3{ec}")
        def store(i=i):
            def f(e):
                return e.dma_start(
                    out=dap(dout, i * TT, [[NT * TT, 128], [128 * NT * TT, NC], [1, TT]]),
                    in_=h2[:])
            return f
        sch.op("sync", store(), waits=[f"{pe}_x3{ec2}" for ec2 in range(NC)],
               inc=sO, event=f"{pe}_out", inc_n=16)
        sch.op("sync", lambda e: None, waits=[f"{pe}_out"],
               inc=sD, event=f"{pe}_outd")

    for _j, _oi in enumerate(_tq_idxs):
        sch.ops[_oi][0] = _P_B0 + 23.5 + _j * 0.001
    _K0 = _tq_idxs[0] - 40.0
    sch.ops[_kvz_idx][0] = _K0 - 0.3
    sch.ops[_kvz_idx + 1][0] = _K0 - 0.2
    sch.ops[_kvev_idx][0] = _K0 - 0.1
    for _j, _oi in enumerate(_b0f[:-8]):
        sch.ops[_oi][0] = _K0 + _j * 0.01


    sch.op("sync", lambda e: None, waits=[f"B{NT-1}_outd"], inc=sD,
           event="all_done")

    with nc.Block() as block:
        @block.sync
        def _(eng):
            sch.emit_engine("sync", eng)

        @block.gpsimd
        def _(eng):
            sch.emit_engine("gpsimd", eng)

        @block.vector
        def _(eng):
            sch.emit_engine("vector", eng)

        @block.scalar
        def _(eng):
            sch.emit_engine("scalar", eng)

        @block.tensor
        def _(eng):
            sch.emit_engine("tensor", eng)

    return nc


def _host_aux():
    ident = np.eye(128)
    msk_ss = np.zeros((NP, 128, 16), dtype=np.float32)
    msk_bc = np.zeros((NP, 16, 128), dtype=np.float32)
    for p in range(NP):
        for r in range(128):
            h = 2 * p + (1 if r >= 64 else 0)
            msk_ss[p, r, h] = 1.0
            msk_bc[p, h, r] = 1.0
    zc = np.zeros((128, 3), dtype=np.float32)
    zc[:, 0] = 1.0
    zc[:, 2] = 1.0
    import ml_dtypes
    bf = ml_dtypes.bfloat16
    return {
        "ident_bf": ident.astype(bf),
        "ident_f": ident.astype(np.float32),
        "ones_f": np.ones((1, 128), dtype=np.float32),
        "zcols_b": zc.astype(bf),
        "msk_ss": msk_ss.astype(bf),
        "msk_bc": msk_bc.astype(bf),
        "sel8": np.concatenate(
            [np.outer((np.arange(8) == r).astype(np.float32),
                      np.ones(128, np.float32)) for r in range(8)],
            axis=1).astype(bf),
        "sel33b": np.stack([(np.arange(33) == 0).astype(np.float32),
                            (np.arange(33) == 32).astype(np.float32)], axis=1),
    }


def kernel(L0_attn_x, L0_attn_Wq, L0_attn_Wk, L0_attn_Wv, L0_attn_Wu,
           L0_attn_Wo, L0_ffn_weight, L0_ffn_bias, L0_c, L0_wmod):
    x = np.ascontiguousarray(np.asarray(L0_attn_x, dtype=np.float32))
    aux = _host_aux()
    shared = {
        "wq_t": np.ascontiguousarray(np.asarray(L0_attn_Wq, np.float32).T),
        "wk_t": np.ascontiguousarray(np.asarray(L0_attn_Wk, np.float32).T),
        "wv_t": np.ascontiguousarray(np.asarray(L0_attn_Wv, np.float32).T),
        "wu_t": np.ascontiguousarray(np.asarray(L0_attn_Wu, np.float32).T),
        "wo_t": np.ascontiguousarray(np.asarray(L0_attn_Wo, np.float32).T),
        "ffn_w": np.ascontiguousarray(np.asarray(L0_ffn_weight, np.float32)),
        "ffn_b": np.ascontiguousarray(
            np.asarray(L0_ffn_bias, np.float32).reshape(8, 128).T),
        "wmod": np.ascontiguousarray(np.asarray(L0_wmod, np.float32)),
        **aux,
    }
    carr = np.asarray(L0_c, np.float32)
    in_maps = []
    for b in range(8):
        m = dict(shared)
        m["x_fm"] = np.ascontiguousarray(x[b].T)
        m["x_tm"] = np.ascontiguousarray(x[b])
        m["c_vec"] = np.ascontiguousarray(carr[b].reshape(8, 128).T)
        in_maps.append(m)
    nc = build_nc()
    res = run_bass_kernel_spmd(nc, in_maps, core_ids=list(range(8)))
    out = np.empty((8, T, D), dtype=np.float32)
    for b in range(8):
        o = np.asarray(res.results[b]["out_fm"])
        out[b] = o.reshape(D, NT * TT).T
    return out
